# version 01 (final)
# baseline (speedup 1.0000x reference)
import numpy as np
import ml_dtypes
from contextlib import ExitStack
import concourse.bass as bass
import concourse.mybir as mybir
from concourse.bass_utils import run_bass_kernel_spmd

F32 = mybir.dt.float32
BF16 = mybir.dt.bfloat16
U8 = mybir.dt.uint8
ALU = mybir.AluOpType
ACT = mybir.ActivationFunctionType
AX = mybir.AxisListType
BF = ml_dtypes.bfloat16

D = 1024
DC = 8
TOK = 2048
LN_EPS = 1e-5
SW = 1027
MW = 515
SCOLS = 2 * SW
MCOLS = 4 * MW
C05 = float(3 * 2 ** 21)
C10 = float(3 * 2 ** 22)


class SemT:
    def __init__(self, nc, es, name):
        self.h = es.enter_context(nc.semaphore(name))
        self.n = 0
    def inc(self, bi, k=1):
        bi.then_inc(self.h, k); self.n += k; return bi
    def dma(self, bi):
        return self.inc(bi, 16)
    def wait(self, eng):
        eng.wait_ge(self.h, self.n)
    def wait_at(self, eng, val):
        eng.wait_ge(self.h, val)


class MK:
    def __init__(self, dbg=()):
        self.dbg = set(dbg)
        self.nc = bass.Bass("TRN2", target_bir_lowering=False)
        self.es = ExitStack(); self.es.__enter__()
        self.dbg_out = {}

    def din(self, name, shape, dt=F32):
        return self.nc.dram_tensor(name, shape, dt, kind="ExternalInput").ap()
    def dout(self, name, shape, dt=F32):
        return self.nc.dram_tensor(name, shape, dt, kind="ExternalOutput").ap()
    def dscr(self, name, shape, dt=F32):
        return self.nc.dram_tensor(name, shape, dt)
    def sb(self, name, shape, dt=F32):
        return self.es.enter_context(self.nc.sbuf_tensor(name, shape, dt))
    def sem(self, name):
        return SemT(self.nc, self.es, name)

    def dbg_emit(self, name, src_ap, shape, dt=F32):
        if name not in self.dbg:
            return
        t = self.dout("dbg_" + name, shape, dt)
        self.dbg_out[name] = t
        gd = self.gd
        def _(g):
            for s_ in self._dbg_waits():
                s_.wait(g)
            gd.dma(g.dma_start(out=t, in_=src_ap))
        self.blk.gpsimd(_)

    def _dbg_waits(self):
        return [self.vv, self.sc, self.mmS, self.gg, self.ld]

    def build(self):
        sb = self.sb
        self.xs = self.din("xs", [D, SCOLS])
        self.xm = self.din("xm", [D, MCOLS])
        self.y_out = self.dout("y_out", [D, TOK])
        WIN = {}
        for name, shape in [
            ("ln1_w", [128, 8]), ("ln1_b", [128, 8]), ("lns_w", [128, 8]), ("lns_b", [128, 8]),
            ("ln2_w", [128, 8]), ("ln2_b", [128, 8]),
            ("conv_w", [128, 32]), ("conv_b", [128, 8]),
            ("in_w", [D, 2048]), ("in_b", [128, 16]),
            ("xp_w", [D, 16]), ("xp_b", [16, 1]), ("A_log", [16, 1]), ("D_mat", [16, D]),
            ("out_w", [D, D]), ("out_b", [128, 8]),
            ("mlp_w1", [D, 2048]), ("mlp_b1", [128, 16]), ("mlp_w2", [2048, D]), ("mlp_b2", [128, 8]),
            ("wg", [2816, D]), ("wu", [2816, D]), ("wd", [D, 2816]),
            ("WkT", [D, D]), ("WvT", [D, D]), ("WqT", [D, D]),
            ("router_w", [D, 8]), ("w1e", [8 * D, D]), ("w2e", [8 * D, D]),
            ("sgu", [D, 2048]), ("sdw", [D, D]), ("sgw", [D, 1]),
        ]:
            WIN[name] = self.din("w_" + name, shape)
        self.WIN = WIN
        self.CIN = {
            "idn": self.din("c_idn", [128, 128]),
            "onesb": self.din("c_onesb", [128, 1], BF16),
            "tpow": self.din("c_tpow", [16, 512]),
            "hsel": self.din("c_hsel", [16, 4 * 32]),
            "rowmask": self.din("c_rowmask", [8, 1]),
            "onesf": self.din("c_onesf", [128, 1]),
            "idnb": self.din("c_idnb", [128, 128], BF16),
            "eps": self.din("c_eps", [128, 1]),
            "padmask": self.din("c_padmask", [128, 18]),
            "xpw": self.din("c_xpw", [128, 128]),
            "rw": self.din("c_rw", [128, 64]),
            "sgw": self.din("c_sgw", [128, 8]),
        }
        self.t2m_spill = self.dscr("t2m_spill", [128, DC * TOK])
        self.z_spill = self.dscr("z_spill", [128, DC * TOK])
        self.xh4_hi_sp = self.dscr("xh4_hi_sp", [128, DC * TOK], BF16)
        self.xh4_lo_sp = self.dscr("xh4_lo_sp", [128, DC * TOK], BF16)
        self.row_scr = self.dscr("row_scr", [8, 2080])
        self.sum_scr = self.dscr("sum_scr", [4, 16])
        self.exch_in = self.dscr("exch_in", [8, 64])
        self.exch_out = self.dscr("exch_out", [8, 64])
        self.u_spill = self.dscr("u_spill", [16, 2048])
        self.xps_spill = self.dscr("xps_spill", [128, DC * TOK])
        self.bc_dbg = self.dscr("bc_dbg", [128, 4096])
        self.nh_dbg = self.dscr("nh_dbg", [128, DC * TOK], BF16)
        self.hsl_spill = self.dscr("hsl_spill", [16, 2048])
        self.WS = {}
        for name, shape in [
            ("in_w", [D, 2048]), ("out_w", [D, D]), ("mlp_w1", [D, 2048]), ("mlp_w2", [2048, D]),
            ("WkT", [D, D]), ("WvT", [D, D]), ("WqT", [D, D]),
            ("w1e", [8 * D, D]), ("w2e", [8 * D, D]), ("sgu", [D, 2048]), ("sdw", [D, D]),
            ("wgT", [22 * 8 * 128, 128]), ("wuT", [22 * 8 * 128, 128]), ("wdT", [22 * 8 * 128, 128]),
        ]:
            self.WS[name] = (self.dscr("ws_" + name + "_hi", shape, BF16),
                             self.dscr("ws_" + name + "_lo", shape, BF16))
        self.h_sp = self.dscr("h_sp", [128, 8 * 16 * 1024])

        self.X = sb("X", [128, DC * TOK])
        self.T2 = sb("T2", [128, DC * TOK])
        self.BFA = sb("BFA", [128, 12288], BF16)
        self.BFB = sb("BFB", [128, 8192], BF16)
        self.FA = sb("FA", [128, 4608])
        self.MSKT = sb("MSKT", [128, 1024], U8)
        self.SROW = sb("SROW", [1, 2080])
        self.PRM = {}
        for n, shp in [("ln1_w", [128, 8]), ("ln1_b", [128, 8]), ("lns_w", [128, 8]), ("lns_b", [128, 8]),
                       ("ln2_w", [128, 8]), ("ln2_b", [128, 8]), ("conv_w", [128, 32]), ("conv_b", [128, 8]),
                       ("in_b", [128, 16]), ("out_b", [128, 8]), ("mlp_b1", [128, 16]), ("mlp_b2", [128, 8]),
                       ("padmask", [128, 18]), ("rowmask", [8, 1])]:
            self.PRM[n] = sb("prm_" + n, shp)
        self.ONESB = sb("ONESB", [128, 1], BF16)
        self.ONESF = sb("ONESF", [128, 1])
        self.EPS = sb("EPS", [128, 1])
        self.IDN = sb("IDN", [128, 128])
        self.IDNB = sb("IDNB", [128, 128], BF16)
        self.XPWH = sb("XPWH", [128, 128], BF16)
        self.XPWL = sb("XPWL", [128, 128], BF16)
        self.RWH = sb("RWH", [128, 64], BF16)
        self.RWL = sb("RWL", [128, 64], BF16)
        self.SGWF = sb("SGWF", [128, 8])
        self.EX8 = sb("EX8", [8, 64])
        self.EX8B = sb("EX8B", [8, 64])
        self.XPB16 = sb("XPB16", [16, 1])
        self.SC16 = sb("SC16", [16, 256])
        self.WSL = [sb(f"WSL{i}", [128, 1024], BF16) for i in range(4)]
        self.PS = [self.es.enter_context(self.nc.psum_tensor(f"ps{i}", [128, 512], F32)) for i in range(8)]
        self.U512 = self.FA[0:16, 0:512]
        self.DECT = self.FA[0:16, 512:1024]
        self.PW = self.FA[0:16, 1024:1536]
        self.HSL = self.FA[0:16, 1792:3840]
        self.D512 = self.SC16[:, 0:16]
        self.SSEQ = self.SC16[:, 16:48]
        self.HSEQ = self.SC16[:, 48:80]
        self.HIN = self.SC16[:, 80:84]
        self.HINT = self.SC16[:, 84:212]
        self.E16 = self.SC16[:, 212:213]
        self.DECC = self.SC16[:, 213:214]
        self.D512C = self.SC16[:, 214:215]
        self.ALOG = self.SC16[:, 215:216]
        self.DMH = self.BFA[0:16, 8192:9216]
        self.DML = self.BFA[0:16, 9216:10240]

        for s_ in ["ld", "gd", "vv", "sc", "mm", "gg", "ccs", "ws"]:
            setattr(self, s_ if s_ != "mm" else "mmS", self.sem("s_" + s_))

        self.blk = self.es.enter_context(self.nc.Block())
        self.emit_all()
        return self

    def bcast_row(self, sy, src_row_ap, dst_tile_ap, cols, scr_row):
        self.ld.dma(sy.dma_start(out=self.row_scr.ap()[scr_row:scr_row + 1, 0:cols], in_=src_row_ap))
        sy.wait_ge(self.ld.h, self.ld.n)
        rep = bass.AP(self.row_scr, scr_row * 2080, [[0, 128], [1, cols]])
        self.ld.dma(sy.dma_start(out=dst_tile_ap, in_=rep))

    def mm_group(self, te, ps_ap, pairs):
        n = len(pairs)
        for i, (lhsT, rhs) in enumerate(pairs):
            te.matmul(ps_ap, lhsT, rhs, start=(i == 0), stop=(i == n - 1))

    def emit_all(self):
        ph = getattr(self, "phases", "ABCD")
        self.emit_loads()
        self.emit_presplit()
        if "A" in ph:
            self.emit_phaseA()
        if "B" in ph:
            self.emit_phaseB()
        if "C" in ph:
            self.emit_phaseC()
        if "D" in ph:
            self.emit_phaseD()
        self.emit_out()

    def emit_loads(self):
        ld = self.ld; PRM = self.PRM
        def _(sy):
            ld.dma(sy.dma_start(out=self.ONESB[:, :], in_=self.CIN["onesb"]))
            ld.dma(sy.dma_start(out=self.IDN[:, :], in_=self.CIN["idn"]))
            ld.dma(sy.dma_start(out=self.IDNB[:, :], in_=self.CIN["idnb"]))
            for n in PRM:
                src = {"padmask": self.CIN["padmask"], "rowmask": self.CIN["rowmask"]}.get(n, self.WIN.get(n))
                ld.dma(sy.dma_start(out=PRM[n][:, :], in_=src))
            ld.dma(sy.dma_start(out=self.ALOG, in_=self.WIN["A_log"]))
            ld.dma(sy.dma_start(out=self.XPB16[:, :], in_=self.WIN["xp_b"]))
            ld.dma(sy.dma_start(out=self.ONESF[:, :], in_=self.CIN["onesf"]))
            ld.dma(sy.dma_start(out=self.EPS[:, :], in_=self.CIN["eps"]))
        self.blk.sync(_)
        def _(sy):
            ld.dma(sy.dma_start(out=self.FA[:, 0:128], in_=self.CIN["xpw"]))
            ld.dma(sy.dma_start(out=self.FA[:, 128:192], in_=self.CIN["rw"]))
            ld.dma(sy.dma_start(out=self.SGWF[:, :], in_=self.CIN["sgw"]))
        self.blk.sync(_)
        def _(v):
            self.ld.wait(v)
            v.tensor_copy(self.XPWH[:, :], self.FA[:, 0:128])
            v.tensor_sub(self.XPWL[:, :], self.FA[:, 0:128], self.XPWH[:, :])
            v.tensor_copy(self.RWH[:, :], self.FA[:, 128:192])
            v.tensor_sub(self.RWL[:, :], self.FA[:, 128:192], self.RWH[:, :])
            self.vv.inc(v.drain())
        self.blk.vector(_)
        self._xpw_vv = self.vv.n

    def emit_quant(self, v, src, dst, Ccols, scr0):
        FA = self.FA
        nb = Ccols // 16
        xn = FA[:, scr0 + 0 * 1040: scr0 + 0 * 1040 + Ccols]
        r05 = FA[:, scr0 + 1 * 1040: scr0 + 1 * 1040 + Ccols]
        r1 = FA[:, scr0 + 2 * 1040: scr0 + 2 * 1040 + Ccols]
        aab = FA[:, scr0 + 3 * 1040: scr0 + 3 * 1040 + Ccols]
        bm = FA[:, 4160: 4160 + nb]
        scl = FA[:, 4224: 4224 + nb]
        rcp = FA[:, 4288: 4288 + nb]
        sgn = dst
        v.tensor_reduce(bm, src.rearrange("p (n l) -> p n l", l=16), AX.X, ALU.max,
                        apply_absolute_value=True)
        v.drain()
        v.tensor_scalar(scl, bm, 1.0 / 6.0, 1e-12, ALU.mult, ALU.max)
        v.drain()
        v.reciprocal(rcp, scl)
        v.drain()
        rcb = rcp.rearrange("p n -> p n", n=nb).unsqueeze(2).broadcast_to([128, nb, 16])
        scb = scl.rearrange("p n -> p n", n=nb).unsqueeze(2).broadcast_to([128, nb, 16])
        v.tensor_mul(xn.rearrange("p (n l) -> p n l", l=16), src.rearrange("p (n l) -> p n l", l=16), rcb)
        v.drain()
        v.tensor_scalar(r05, xn, C05, -C05, ALU.add, ALU.add)
        v.tensor_scalar(r1, xn, C10, -C10, ALU.add, ALU.add)
        v.tensor_mul(aab, xn, xn)
        v.tensor_single_scalar(sgn, xn, 0.0, ALU.is_gt)
        v.drain()
        v.tensor_scalar(sgn, sgn, 2.0, -1.0, ALU.mult, ALU.add)
        v.tensor_single_scalar(xn, aab, 25.0, ALU.is_gt)
        v.drain()
        v.tensor_scalar(xn, xn, 2.0, 4.0, ALU.mult, ALU.add)
        v.drain()
        v.tensor_mul(xn, xn, sgn)
        v.tensor_single_scalar(self.MSKT[:, 0:Ccols], aab, 5.0625, ALU.is_gt)
        v.drain()
        v.select(r05, self.MSKT[:, 0:Ccols], r1, r05)
        v.drain()
        v.tensor_single_scalar(self.MSKT[:, 0:Ccols], aab, 20.25, ALU.is_gt)
        v.drain()
        v.select(r05, self.MSKT[:, 0:Ccols], xn, r05)
        v.drain()
        v.tensor_mul(dst.rearrange("p (n l) -> p n l", l=16), r05.rearrange("p (n l) -> p n l", l=16), scb)

    def emit_transposes(self, blocks, idn, combine="copy", extra_te_wait=None, extra_v_wait=None):
        mmS = self.mmS; vv = self.vv
        base_mm = mmS.n; base_vv = vv.n
        idn_ap = idn[:, :] if not isinstance(idn, bass.AP) else idn
        def _t(te):
            te.wait_ge(vv.h, base_vv)
            if extra_te_wait is not None:
                extra_te_wait(te)
            for i, (srcap, dstap) in enumerate(blocks):
                if i >= 4:
                    te.wait_ge(vv.h, base_vv + (i - 3))
                te.transpose(self.PS[4 + i % 4][:, 0:128], srcap, idn_ap)
                self.mmS.inc(te.drain())
        self.blk.tensor(_t)
        def _v(v):
            if extra_v_wait is not None:
                extra_v_wait(v)
            for i, (srcap, dstap) in enumerate(blocks):
                v.wait_ge(mmS.h, base_mm + i + 1)
                if combine == "copy":
                    v.tensor_copy(dstap, self.PS[4 + i % 4][:, 0:128])
                else:
                    v.tensor_add(dstap, dstap, self.PS[4 + i % 4][:, 0:128])
                self.vv.inc(v.drain())
        self.blk.vector(_v)

    def emit_presplit(self):
        ld = self.ld; vv = self.vv; sc = self.sc; ws = self.ws
        X, BFA, BFB = self.X, self.BFA, self.BFB
        QD, QT = 14080, 8448
        jobs = []
        for name, R, Ccols in [("in_w", D, 2048), ("out_w", D, D), ("mlp_w1", D, 2048),
                               ("mlp_w2", 2048, D), ("WkT", D, D), ("WvT", D, D), ("WqT", D, D),
                               ("w1e", 8 * D, D), ("w2e", 8 * D, D), ("sgu", D, 2048), ("sdw", D, D)]:
            for r0 in range(0, R, 128):
                jobs.append((name, None, r0, Ccols, False))
        if not getattr(self, "skip_quant", False):
            for name, tname, R, Ccols in [("wg", "wgT", 2816, D), ("wu", "wuT", 2816, D),
                                          ("wd", "wdT", D, 2816)]:
                for r0 in range(0, R, 128):
                    jobs.append((name, tname, r0, Ccols, True))
        only = getattr(self, "presplit_only", None)
        if only is not None:
            jobs = [j for j in jobs if j[0] in only]
        vvdone = []
        wsdone = []
        for j, (name, tname, r0, Ccols, q) in enumerate(jobs):
            s = j % 2
            def _l(sy, j=j, name=name, r0=r0, Ccols=Ccols, s=s):
                if j >= 2:
                    sy.wait_ge(vv.h, vvdone[j - 2])
                ld.dma(sy.dma_start(out=X[:, s * 2816: s * 2816 + Ccols],
                                    in_=self.WIN[name][r0:r0 + 128, :]))
            self.blk.sync(_l)
            ld_after = ld.n
            if not q:
                def _hi(se, j=j, Ccols=Ccols, s=s, ld_after=ld_after):
                    se.wait_ge(ld.h, ld_after)
                    if j >= 2:
                        se.wait_ge(ws.h, wsdone[j - 2])
                    se.copy(BFA[:, s * 2816: s * 2816 + Ccols], X[:, s * 2816: s * 2816 + Ccols])
                    self.sc.inc(se.drain())
                self.blk.scalar(_hi)
                sc_after = sc.n
                def _lo(v, j=j, Ccols=Ccols, s=s, ld_after=ld_after, sc_after=sc_after):
                    v.wait_ge(ld.h, ld_after)
                    v.wait_ge(sc.h, sc_after)
                    if j >= 2:
                        v.wait_ge(ws.h, wsdone[j - 2])
                    v.tensor_sub(BFB[:, s * 2816: s * 2816 + Ccols], X[:, s * 2816: s * 2816 + Ccols],
                                 BFA[:, s * 2816: s * 2816 + Ccols])
                    self.vv.inc(v.drain())
                self.blk.vector(_lo)
            else:
                for p0 in range(0, Ccols, 1024):
                    pw = min(1024, Ccols - p0)
                    def _q(v, s=s, p0=p0, pw=pw, ld_after=ld_after):
                        if p0 == 0:
                            v.wait_ge(ld.h, ld_after)
                        self.emit_quant(v, X[:, s * 2816 + p0: s * 2816 + p0 + pw],
                                        X[:, QD: QD + pw], pw, 0)
                        self.vv.inc(v.drain())
                    self.blk.vector(_q)
                    blocks = [(X[:, QD + jb * 128: QD + (jb + 1) * 128],
                               X[:, QT + p0 + jb * 128: QT + p0 + (jb + 1) * 128])
                              for jb in range(pw // 128)]
                    self.emit_transposes(blocks, self.IDN)
                def _sp(v, j=j, s=s, Ccols=Ccols):
                    if j >= 2:
                        v.wait_ge(ws.h, wsdone[j - 2])
                    v.tensor_copy(BFA[:, s * 2816: s * 2816 + Ccols], X[:, QT: QT + Ccols])
                    v.tensor_sub(BFB[:, s * 2816: s * 2816 + Ccols], X[:, QT: QT + Ccols],
                                 BFA[:, s * 2816: s * 2816 + Ccols])
                    self.vv.inc(v.drain())
                self.blk.vector(_sp)
            vvdone.append(vv.n)
            def _st(g, j=j, name=name, tname=tname, r0=r0, Ccols=Ccols, s=s, q=q):
                g.wait_ge(vv.h, vvdone[j])
                if not q:
                    hi, lo = self.WS[name]
                    ws.dma(g.dma_start(out=hi.ap()[r0:r0 + 128, :], in_=BFA[:, s * 2816: s * 2816 + Ccols]))
                    ws.dma(g.dma_start(out=lo.ap()[r0:r0 + 128, :], in_=BFB[:, s * 2816: s * 2816 + Ccols]))
                elif tname == "wdT":
                    hi, lo = self.WS[tname]
                    db = r0 // 128
                    for jb in range(Ccols // 128):
                        rr0 = (jb * 8 + db) * 128
                        ws.dma(g.dma_start(out=hi.ap()[rr0:rr0 + 128, :],
                                           in_=BFA[:, s * 2816 + jb * 128: s * 2816 + (jb + 1) * 128]))
                        ws.dma(g.dma_start(out=lo.ap()[rr0:rr0 + 128, :],
                                           in_=BFB[:, s * 2816 + jb * 128: s * 2816 + (jb + 1) * 128]))
                else:
                    hi, lo = self.WS[tname]
                    fb = r0 // 128
                    for jb in range(Ccols // 128):
                        rr0 = (fb * 8 + jb) * 128
                        ws.dma(g.dma_start(out=hi.ap()[rr0:rr0 + 128, :],
                                           in_=BFA[:, s * 2816 + jb * 128: s * 2816 + (jb + 1) * 128]))
                        ws.dma(g.dma_start(out=lo.ap()[rr0:rr0 + 128, :],
                                           in_=BFB[:, s * 2816 + jb * 128: s * 2816 + (jb + 1) * 128]))
            self.blk.gpsimd(_st)
            wsdone.append(ws.n)
        self.n_presplit_jobs = len(jobs)
        if "wq" in self.dbg:
            gd = self.gd
            outs = {}
            dump_names = [("wgT", [22528, 128]), ("wuT", [22528, 128]), ("wdT", [22528, 128])]
            if only is not None:
                dump_names = [(n, s) for n, s in dump_names if n[:-1] in only or n in only]
            for nm, shape in dump_names:
                for hl in range(2):
                    outs[(nm, hl)] = self.dout(f"dbg_{nm}_{hl}", shape, BF16)
            def _wd(g):
                g.wait_ge(ws.h, ws.n)
                for (nm, hl), t in outs.items():
                    gd.dma(g.dma_start(out=t, in_=self.WS[nm][hl].ap()))
            self.blk.gpsimd(_wd)
            for (nm, hl), t in outs.items():
                self.dbg_out[f"{nm}_{hl}"] = t

    def emit_scan_setup(self):
        sc = self.sc; vv = self.vv; ld = self.ld
        def _(sy):
            self.vv.wait(sy); self.sc.wait(sy)
            ld.dma(sy.dma_start(out=self.PW, in_=self.CIN["tpow"]))
            ld.dma(sy.dma_start(out=self.U512.transpose([0, 1]) if False else self.FA[0:16, 1536:1664],
                                in_=self.CIN["hsel"]))
        self.blk.sync(_)
        self.HSELC = self.FA[0:16, 1536:1664]
        def _(se):
            self.ld.wait(se)
            se.activation(self.E16, self.ALOG, ACT.Exp)
            se.activation(self.DECC, self.E16, ACT.Exp, scale=-1.0)
            se.activation(self.D512C, self.E16, ACT.Exp, scale=-512.0)
            self.sc.inc(se.drain())
        self.blk.scalar(_)
        def _(v):
            self.sc.wait(v)
            v.tensor_copy(self.DECT, self.DECC.broadcast_to([16, 512]))
            v.tensor_copy(self.D512, self.D512C.broadcast_to([16, 16]))
            v.tensor_scalar_mul(self.U512, self.PW, self.E16)
            self.vv.inc(v.drain())
        self.blk.vector(_)
        def _(se):
            self.vv.wait(se)
            se.activation(self.PW, self.U512, ACT.Exp, scale=-1.0)
            self.sc.inc(se.drain())
        self.blk.scalar(_)

    def sc_op(self, se, bi):
        se.drain()
        self.sc.inc(se.drain())
        return bi

    def emit_phaseA(self):
        self.emit_front(stream=True)
        self.dbg_emit("t2s", self.T2[:, :], [128, DC * TOK])
        self.emit_front(stream=False)
        self.dbg_emit("t2m", self.t2m_spill.ap(), [128, DC * TOK])
        self.dbg_emit("bcm2", self.X[:, 12288:12288 + 2048], [128, 2048])
        self.dbg_emit("bcr2", self.X[:, 14336:14336 + 2048], [128, 2048])
        self.emit_scan_setup()
        self.dbg_emit("sc16", self.SC16[:, :], [16, 256])
        self.dbg_emit("dect", self.DECT, [16, 512])
        self.dbg_emit("pw", self.PW, [16, 512])
        self.emit_inproj()
        self.dbg_emit("u", self.X[0:16, 0:2048] if False else self.HSL, [16, 2048])
        self.dbg_emit("zsp", self.z_spill.ap(), [128, DC * TOK])
        self.emit_exchange()
        self.dbg_emit("sc16b", self.SC16[:, :], [16, 256])
        self.dbg_emit("usp", self.u_spill.ap(), [16, 2048])
        self.dbg_emit("xps", self.xps_spill.ap(), [128, DC * TOK])
        self.dbg_emit("bcd", self.bc_dbg.ap(), [128, 4096])
        self.dbg_emit("nhd", self.nh_dbg.ap(), [128, DC * TOK], BF16)
        self.dbg_emit("hslsp", self.hsl_spill.ap(), [16, 2048])
        if "wih" in self.dbg:
            t1 = self.dout("dbg_wih", [D, 2048], BF16)
            t2 = self.dout("dbg_wil", [D, 2048], BF16)
            gd = self.gd
            def _wd(g):
                for s_ in self._dbg_waits():
                    s_.wait(g)
                g.wait_ge(self.ws.h, self.ws.n)
                gd.dma(g.dma_start(out=t1, in_=self.WS["in_w"][0].ap()))
                gd.dma(g.dma_start(out=t2, in_=self.WS["in_w"][1].ap()))
            self.blk.gpsimd(_wd)
            self.dbg_out["wih"] = t1; self.dbg_out["wil"] = t2
        self.dbg_emit("hsc", self.HSL, [16, 2048])
        self.emit_ys_outproj()
        self.dbg_emit("satt", self.T2[:, :], [128, DC * TOK])
        self.emit_ln2_mlp()

    XCH, BCRP, ACH, CVCH, TCH, BCMX, BCRX = 0, 2080, 4160, 6240, 8320, 10400, 12480
    XT0 = CVCH
    TMPT, XPSs, ZSBs, BCM2, BCR2 = 0, 4096, 8192, 12288, 14336

    def emit_front(self, stream):
        ld = self.ld; vv = self.vv; sc = self.sc; mmS = self.mmS; gd = self.gd
        X = self.X; T2 = self.T2; BFA = self.BFA; FA = self.FA
        cols = SCOLS if stream else MCOLS
        nruns = 2 if stream else 4
        W = SW if stream else MW
        L = W - 3
        xin = self.xs if stream else self.xm
        ngr = (cols + 511) // 512
        base_ld = ld.n; base_sc = sc.n; base_mm = mmS.n; base_vv = vv.n
        def _l(sy):
            for g in range(ngr):
                g0 = g * 512; w = min(512, cols - g0)
                if g >= 2:
                    sy.wait_ge(vv.h, base_vv + (g - 1))
                for c in range(DC):
                    if g * DC + c >= 4:
                        sy.wait_ge(mmS.h, base_mm + (g * DC + c - 3))
                    elif g == 0 and c == 0:
                        sy.wait_ge(mmS.h, base_mm)
                        sy.wait_ge(vv.h, base_vv)
                    ld.dma(sy.dma_start(out=X[:, self.XT0 + (c % 4) * 512: self.XT0 + (c % 4) * 512 + w],
                                        in_=xin[c * 128:(c + 1) * 128, g0:g0 + w]))
        self.blk.sync(_l)
        def _s(se):
            for g in range(ngr):
                g0 = g * 512; w = min(512, cols - g0)
                for c in range(DC):
                    se.wait_ge(ld.h, base_ld + 16 * (g * DC + c + 1))
                    if g * DC + c >= 4:
                        se.wait_ge(mmS.h, base_mm + (g * DC + c - 3))
                    elif g == 0 and c == 0:
                        se.wait_ge(mmS.h, base_mm)
                        se.wait_ge(self.ws.h, self.ws.n)
                    se.activation(BFA[:, (c % 4) * 512: (c % 4) * 512 + w],
                                         X[:, self.XT0 + (c % 4) * 512: self.XT0 + (c % 4) * 512 + w],
                                         ACT.Square)
                    self.sc.inc(se.drain())
        self.blk.scalar(_s)
        def _t(te):
            for g in range(ngr):
                g0 = g * 512; w = min(512, cols - g0)
                px, pq = self.PS[2 * (g % 2)], self.PS[2 * (g % 2) + 1]
                if g >= 2:
                    te.wait_ge(vv.h, base_vv + (g - 1))
                for c in range(DC):
                    te.wait_ge(ld.h, base_ld + 16 * (g * DC + c + 1))
                    te.wait_ge(sc.h, base_sc + min(g * DC + c + 2, ngr * DC))
                    xt = X[:, self.XT0 + (c % 4) * 512: self.XT0 + (c % 4) * 512 + w]
                    sq = BFA[:, (c % 4) * 512: (c % 4) * 512 + w]
                    te.matmul(px[0:1, 0:w], self.ONESF[:, 0:1], xt, start=(c == 0), stop=(c == DC - 1))
                    bi = te.matmul(pq[0:1, 0:w], self.ONESB[:, 0:1], sq, start=(c == 0), stop=(c == DC - 1))
                    mmS.inc(bi)
        self.blk.tensor(_t)
        def _v(v):
            for g in range(ngr):
                g0 = g * 512; w = min(512, cols - g0)
                px, pq = self.PS[2 * (g % 2)], self.PS[2 * (g % 2) + 1]
                if g == 0:
                    v.wait_ge(ld.h, base_ld)
                v.wait_ge(mmS.h, base_mm + DC * (g + 1))
                v.tensor_copy(self.SROW[0:1, g0:g0 + w], px[0:1, 0:w])
                v.tensor_copy(FA[0:1, g0:g0 + w], pq[0:1, 0:w])
                self.vv.inc(v.drain())
        self.blk.vector(_v)
        mx = self.SROW[0:1, 0:cols]
        vx = FA[0:1, 0:cols]
        rx = FA[0:1, 2080:2080 + cols]
        rp = X[0:1, self.XCH: self.XCH + cols]
        def _v(v):
            vv.wait(v)
            v.tensor_scalar_mul(mx, mx, 1.0 / 1024.0)
            v.tensor_scalar_mul(vx, vx, 1.0 / 1024.0)
            v.tensor_mul(rx, mx, mx)
            v.tensor_sub(vx, vx, rx)
            self.vv.inc(v.drain())
        self.blk.vector(_v)
        tmr = X[0:1, self.TCH: self.TCH + cols]
        def _s(se):
            vv.wait(se)
            se.activation(tmr, vx, ACT.Sqrt, bias=self.EPS[0:1, 0:1])
            self.sc.inc(se.drain())
        self.blk.scalar(_s)
        def _v(v):
            sc.wait(v)
            v.reciprocal(rx, tmr)
            v.tensor_mul(rp, rx, rx)
            v.tensor_mul(rp, rp, vx)
            self.vv.inc(v.drain())
        self.blk.vector(_v)
        def _s(se):
            vv.wait(se)
            se.activation(tmr, rp, ACT.Sqrt, bias=self.EPS[0:1, 0:1])
            self.sc.inc(se.drain())
        self.blk.scalar(_s)
        def _v(v):
            sc.wait(v)
            v.reciprocal(rp, tmr)
            self.vv.inc(v.drain())
        self.blk.vector(_v)
        def _b(sy):
            vv.wait(sy)
            self.bcast_row(sy, mx, X[:, self.BCMX: self.BCMX + cols], cols, 0)
            self.bcast_row(sy, rx, X[:, self.BCRX: self.BCRX + cols], cols, 1)
            self.bcast_row(sy, rp, X[:, self.BCRP: self.BCRP + cols], cols, 2)
        self.blk.sync(_b)
        if stream:
            self.dbg_emit("bcmx", X[:, self.BCMX: self.BCMX + cols], [128, cols])
            self.dbg_emit("bcrx", X[:, self.BCRX: self.BCRX + cols], [128, cols])
            self.dbg_emit("bcrp", X[:, self.BCRP: self.BCRP + cols], [128, cols])
            self.dbg_emit("srow", self.SROW[0:1, 0:cols], [1, cols])
        base_ld2 = ld.n; base_vv2 = vv.n; base_sc2 = sc.n; base_mm2 = mmS.n; base_gd2 = gd.n
        pml = self.PRM["padmask"]
        pmoff = 0 if stream else 6
        def _l(sy):
            for c in range(DC):
                if c >= 1:
                    sy.wait_ge(vv.h, base_vv2 + c)
                else:
                    sy.wait_ge(ld.h, base_ld2)
                ld.dma(sy.dma_start(out=X[:, self.XCH: self.XCH + cols], in_=xin[c * 128:(c + 1) * 128, :]))
        self.blk.sync(_l)
        def _v(v):
            for c in range(DC):
                v.wait_ge(ld.h, base_ld2 + 16 * (c + 1))
                if (not stream) and c >= 1:
                    v.wait_ge(gd.h, base_gd2 + 16 * c)
                    v.wait_ge(mmS.h, base_mm2 + c)
                xch = X[:, self.XCH: self.XCH + cols]
                tch = X[:, self.TCH: self.TCH + cols]
                ach = X[:, self.ACH: self.ACH + cols]
                v.tensor_sub(tch, xch, X[:, self.BCMX: self.BCMX + cols])
                v.tensor_mul(tch, tch, X[:, self.BCRX: self.BCRX + cols])
                v.tensor_mul(ach, tch, X[:, self.BCRP: self.BCRP + cols])
                v.tensor_scalar(ach, ach, self.PRM["ln1_w"][:, c:c + 1], self.PRM["ln1_b"][:, c:c + 1],
                                ALU.mult, ALU.add)
                achv = ach.rearrange("p (r w) -> p r w", r=nruns)
                v.tensor_mul(achv[:, :, 0:3], achv[:, :, 0:3],
                             pml[:, pmoff:pmoff + 3 * nruns].rearrange("p (r k) -> p r k", r=nruns))
                cv = X[:, self.CVCH: self.CVCH + nruns * L].rearrange("p (r w) -> p r w", r=nruns)
                v.tensor_scalar(cv, achv[:, :, 0:L], self.PRM["conv_w"][:, 4 * c: 4 * c + 1],
                                self.PRM["conv_b"][:, c:c + 1], ALU.mult, ALU.add)
                for k in range(1, 4):
                    v.scalar_tensor_tensor(cv, achv[:, :, k:k + L], self.PRM["conv_w"][:, 4 * c + k: 4 * c + k + 1],
                                           cv, ALU.mult, ALU.add)
                tin = tch.rearrange("p (r w) -> p r w", r=nruns)[:, :, 3:W]
                if stream:
                    dst = T2[:, c * 2048:(c + 1) * 2048].rearrange("p (r w) -> p r w", r=nruns)
                else:
                    dst = ach[:, 0:nruns * L].rearrange("p (r w) -> p r w", r=nruns)
                v.tensor_add(dst, tin, cv)
                self.vv.inc(v.drain())
        self.blk.vector(_v)
        if not stream:
            def _s(se):
                for c in range(DC):
                    se.wait_ge(vv.h, base_vv2 + (c + 1))
                    if c >= 2:
                        se.wait_ge(mmS.h, base_mm2 + c - 1)
                    se.activation(BFA[:, (c % 2) * 2048:(c % 2) * 2048 + 2048],
                                         X[:, self.ACH: self.ACH + 2048], ACT.Square)
                    self.sc.inc(se.drain())
            self.blk.scalar(_s)
            def _t(te):
                for c in range(DC):
                    te.wait_ge(vv.h, base_vv2 + (c + 1))
                    te.wait_ge(sc.h, base_sc2 + (c + 1))
                    for t in range(4):
                        te.matmul(self.PS[t][0:1, :], self.ONESF[:, 0:1],
                                  X[:, self.ACH + t * 512: self.ACH + (t + 1) * 512],
                                  start=(c == 0), stop=(c == DC - 1))
                        bi = te.matmul(self.PS[4 + t][0:1, :], self.ONESB[:, 0:1],
                                       BFA[:, (c % 2) * 2048 + t * 512:(c % 2) * 2048 + (t + 1) * 512],
                                       start=(c == 0), stop=(c == DC - 1))
                    mmS.inc(bi)
            self.blk.tensor(_t)
            def _g(g):
                for c in range(DC):
                    g.wait_ge(sc.h, base_sc2 + (c + 1))
                    g.wait_ge(mmS.h, base_mm2 + (c + 1))
                    gd.dma(g.dma_start(out=self.t2m_spill.ap()[:, c * 2048:(c + 1) * 2048],
                                       in_=X[:, self.ACH: self.ACH + 2048]))
            self.blk.gpsimd(_g)
            def _v(v):
                v.wait_ge(mmS.h, base_mm2 + DC)
                for t in range(4):
                    v.tensor_copy(self.SROW[0:1, t * 512:(t + 1) * 512], self.PS[t][0:1, :])
                    v.tensor_copy(FA[0:1, t * 512:(t + 1) * 512], self.PS[4 + t][0:1, :])
                m2 = self.SROW[0:1, 0:2048]
                v2 = FA[0:1, 0:2048]
                r2t = FA[0:1, 2080:2080 + 2048]
                v.tensor_scalar_mul(m2, m2, 1.0 / 1024.0)
                v.tensor_scalar_mul(v2, v2, 1.0 / 1024.0)
                v.tensor_mul(r2t, m2, m2)
                v.tensor_sub(v2, v2, r2t)
                self.vv.inc(v.drain())
            self.blk.vector(_v)
            def _s(se):
                vv.wait(se)
                se.activation(X[0:1, self.TCH: self.TCH + 2048], FA[0:1, 0:2048],
                                     ACT.Sqrt, bias=self.EPS[0:1, 0:1])
                self.sc.inc(se.drain())
            self.blk.scalar(_s)
            def _v(v):
                sc.wait(v)
                v.reciprocal(FA[0:1, 2080:2080 + 2048], X[0:1, self.TCH: self.TCH + 2048])
                self.vv.inc(v.drain())
            self.blk.vector(_v)
            def _b(sy):
                vv.wait(sy)
                self.bcast_row(sy, self.SROW[0:1, 0:2048], X[:, self.BCM2: self.BCM2 + 2048], 2048, 3)
                self.bcast_row(sy, FA[0:1, 2080:2080 + 2048], X[:, self.BCR2: self.BCR2 + 2048], 2048, 4)
            self.blk.sync(_b)

    def emit_inproj(self):
        ld = self.ld; vv = self.vv; sc = self.sc; mmS = self.mmS; gd = self.gd
        X = self.X; BFA = self.BFA; BFB = self.BFB
        wh, wl = self.WS["in_w"]
        for t in range(4):
            base_ld = ld.n; base_vv = vv.n; base_sc = sc.n; base_mm = mmS.n
            base_gd_t = gd.n
            def _l(sy, t=t):
                vv.wait(sy); sc.wait(sy); gd.wait(sy)
                for cch in range(DC):
                    ld.dma(sy.dma_start(
                        out=X[:, self.TMPT + cch * 512: self.TMPT + (cch + 1) * 512],
                        in_=self.t2m_spill.ap()[:, cch * 2048 + t * 512: cch * 2048 + (t + 1) * 512]))
            self.blk.sync(_l)
            def _v(v, t=t):
                v.wait_ge(ld.h, base_ld + 16 * DC)
                tm = X[:, self.TMPT: self.TMPT + 4096].rearrange("p (c w) -> p c w", c=DC)
                bm = X[:, self.BCM2 + t * 512: self.BCM2 + (t + 1) * 512].unsqueeze(1).broadcast_to([128, DC, 512])
                br = X[:, self.BCR2 + t * 512: self.BCR2 + (t + 1) * 512].unsqueeze(1).broadcast_to([128, DC, 512])
                aux = X[:, self.XPSs: self.XPSs + 4096].rearrange("p (c w) -> p c w", c=DC)
                if getattr(self, "bypass_nhat", False):
                    v.tensor_copy(BFA[:, 0:4096], X[:, self.TMPT: self.TMPT + 4096])
                    v.tensor_sub(BFB[:, 0:4096], X[:, self.TMPT: self.TMPT + 4096], BFA[:, 0:4096])
                    self.vv.inc(v.drain())
                else:
                    bm2 = X[:, self.BCM2 + t * 512: self.BCM2 + (t + 1) * 512]
                    br2 = X[:, self.BCR2 + t * 512: self.BCR2 + (t + 1) * 512]
                    for c in range(DC):
                        v.tensor_sub(aux[:, c, :], tm[:, c, :], bm2)
                        if getattr(self, "skip_lns_affine", False):
                            v.tensor_mul(aux[:, c, :], tm[:, c, :] if False else aux[:, c, :], br2)
                        else:
                            v.tensor_mul(tm[:, c, :], aux[:, c, :], br2)
                            v.tensor_scalar(aux[:, c, :], tm[:, c, :], self.PRM["lns_w"][:, c:c + 1],
                                            self.PRM["lns_b"][:, c:c + 1], ALU.mult, ALU.add)
                    v.tensor_copy(BFA[:, 0:4096], X[:, self.XPSs: self.XPSs + 4096])
                    v.tensor_sub(BFB[:, 0:4096], X[:, self.XPSs: self.XPSs + 4096], BFA[:, 0:4096])
                    self.vv.inc(v.drain())
            self.blk.vector(_v)
            for fcg in range(2):
                base_ld2 = ld.n
                def _lw(sy, fcg=fcg):
                    for k in range(DC):
                        sy.wait_ge(mmS.h, base_mm + fcg * DC + k)
                        ld.dma(sy.dma_start(out=self.WSL[2 * (k % 2)][:, :],
                                            in_=wh.ap()[k * 128:(k + 1) * 128, fcg * 1024:(fcg + 1) * 1024]))
                        ld.dma(sy.dma_start(out=self.WSL[2 * (k % 2) + 1][:, :],
                                            in_=wl.ap()[k * 128:(k + 1) * 128, fcg * 1024:(fcg + 1) * 1024]))
                self.blk.sync(_lw)
                def _t(te, fcg=fcg):
                    te.wait_ge(vv.h, base_vv + 1)
                    if fcg == 1:
                        te.wait_ge(sc.h, base_sc + 8)
                    for k in range(DC):
                        te.wait_ge(ld.h, base_ld2 + 32 * (k + 1))
                        whs = self.WSL[2 * (k % 2)]
                        wls = self.WSL[2 * (k % 2) + 1]
                        nh = BFA[:, k * 512: (k + 1) * 512]
                        nl = BFB[:, k * 512: (k + 1) * 512]
                        for fc in range(8):
                            psb = self.PS[fc]
                            te.matmul(psb[:, :], whs[:, fc * 128:(fc + 1) * 128], nh, start=(k == 0), stop=False)
                            te.matmul(psb[:, :], wls[:, fc * 128:(fc + 1) * 128], nh, start=False, stop=False)
                            bi = te.matmul(psb[:, :], whs[:, fc * 128:(fc + 1) * 128], nl, start=False,
                                           stop=(k == DC - 1))
                        mmS.inc(bi)
                self.blk.tensor(_t)
                def _s(se, fcg=fcg):
                    se.wait_ge(mmS.h, base_mm + (fcg + 1) * DC)
                    for fc in range(8):
                        fcol = fcg * 8 + fc
                        if fcg == 0:
                            se.activation(X[:, self.XPSs + fc * 512: self.XPSs + (fc + 1) * 512],
                                                 self.PS[fc][:, :], ACT.Silu,
                                                 bias=self.PRM["in_b"][:, fcol:fcol + 1])
                            self.sc.inc(se.drain())
                        else:
                            se.activation(X[:, self.ZSBs + fc * 512: self.ZSBs + (fc + 1) * 512],
                                                 self.PS[fc][:, :], ACT.Sigmoid,
                                                 bias=self.PRM["in_b"][:, fcol:fcol + 1])
                            self.sc.inc(se.drain())
                self.blk.scalar(_s)
            def _v(v, t=t):
                v.wait_ge(sc.h, base_sc + 8)
                v.tensor_copy(BFA[:, 4096:8192], X[:, self.XPSs: self.XPSs + 4096])
                v.tensor_sub(BFB[:, 4096:8192], X[:, self.XPSs: self.XPSs + 4096], BFA[:, 4096:8192])
                self.vv.inc(v.drain())
            self.blk.vector(_v)
            def _t(te, t=t):
                te.wait_ge(vv.h, base_vv + 2)
                te.wait_ge(sc.h, base_sc + 16)
                for k in range(DC):
                    xh = BFA[:, 4096 + k * 512: 4096 + (k + 1) * 512]
                    xl = BFB[:, 4096 + k * 512: 4096 + (k + 1) * 512]
                    te.matmul(self.PS[0][0:16, :], self.XPWH[:, k * 16:(k + 1) * 16], xh, start=(k == 0), stop=False)
                    te.matmul(self.PS[0][0:16, :], self.XPWL[:, k * 16:(k + 1) * 16], xh, start=False, stop=False)
                    bi = te.matmul(self.PS[0][0:16, :], self.XPWH[:, k * 16:(k + 1) * 16], xl, start=False,
                                   stop=(k == DC - 1))
                mmS.inc(bi)
            self.blk.tensor(_t)
            def _s(se, t=t):
                mmS.wait(se)
                se.activation(self.U512, self.PS[0][0:16, :], ACT.Identity, bias=self.XPB16[:, 0:1])
                self.sc.inc(se.drain())
            self.blk.scalar(_s)
            def _v(v, t=t):
                sc.wait(v)
                v.tensor_tensor_scan(self.HSL[:, t * 512:(t + 1) * 512], self.DECT, self.U512,
                                            0.0, ALU.mult, ALU.add)
                self.vv.inc(v.drain())
            self.blk.vector(_v)
            def _g(g, t=t):
                vv.wait(g)
                sc.wait(g)
                for cch in range(DC):
                    gd.dma(g.dma_start(
                        out=self.z_spill.ap()[:, cch * 2048 + t * 512: cch * 2048 + (t + 1) * 512],
                        in_=X[:, self.ZSBs + cch * 512: self.ZSBs + (cch + 1) * 512]))
                gd.dma(g.dma_start(out=self.sum_scr.ap()[t:t + 1, :],
                                   in_=self.HSL[:, t * 512 + 511: t * 512 + 512]))
                gd.dma(g.dma_start(out=self.u_spill.ap()[:, t * 512:(t + 1) * 512], in_=self.U512))

            self.blk.gpsimd(_g)

    def emit_exchange(self):
        gd = self.gd; vv = self.vv; ccs = self.ccs
        def _g(g):
            g.wait_ge(gd.h, gd.n)
            gd.dma(g.dma_start(out=self.EX8[:, :], in_=bass.AP(self.sum_scr, 0, [[0, 8], [1, 64]])))
        self.blk.gpsimd(_g)
        def _v(v):
            self.gd.wait(v)
            v.tensor_scalar_mul(self.EX8B[:, :], self.EX8[:, :], self.PRM["rowmask"][:, 0:1])
            self.vv.inc(v.drain())
        self.blk.vector(_v)
        def _g(g):
            vv.wait(g)
            gd.dma(g.dma_start(out=self.exch_in.ap(), in_=self.EX8B[:, :]))
            g.wait_ge(gd.h, gd.n)
            ccs.inc(g.collective_compute("AllReduce", ALU.add, replica_groups=[list(range(8))],
                                         ins=[self.exch_in.ap().opt()], outs=[self.exch_out.ap().opt()]))
            ccs.wait(g)
            for b in range(2):
                for jh in range(2):
                    src = bass.AP(self.exch_out, b * 16 + jh * 32, [[1, 16], [64, 8]])
                    dstap = bass.AP(self.SC16, 16 + b * 16 + jh * 8, [[256, 16], [1, 8]])
                    gd.dma(g.dma_start(out=dstap, in_=src, allow_slow_non_contiguous=True))
        self.blk.gpsimd(_g)
        self.HSHI = self.BFA[0:16, 10240:12288]
        self.HSLO = self.BFB[0:16, 4096:6144]
        def _v(v):
            self.gd.wait(v)
            v.tensor_tensor_scan(self.HSEQ[:, 0:16], self.D512, self.SSEQ[:, 0:16], 0.0, ALU.mult, ALU.add)
            v.tensor_tensor_scan(self.HSEQ[:, 16:32], self.D512, self.SSEQ[:, 16:32], 0.0, ALU.mult, ALU.add)
            for rr in range(4):
                v.tensor_mul(self.HINT[:, rr * 32:(rr + 1) * 32], self.HSEQ,
                             self.HSELC[:, rr * 32:(rr + 1) * 32])
            vv.inc(v.tensor_reduce(self.HIN, self.HINT.rearrange("n (r j) -> n r j", r=4), AX.X, ALU.add))
            vv.wait(v)
            for r in range(4):
                v.scalar_tensor_tensor(self.HSL[:, r * 512:(r + 1) * 512], self.PW, self.HIN[:, r:r + 1],
                                       self.HSL[:, r * 512:(r + 1) * 512], ALU.mult, ALU.add)
            v.tensor_copy(self.HSHI, self.HSL)
            v.tensor_sub(self.HSLO, self.HSL, self.HSHI)
            self.vv.inc(v.drain())
        self.blk.vector(_v)
        def _g2(g):
            self.vv.wait(g)
            self.gd.dma(g.dma_start(out=self.hsl_spill.ap(), in_=self.HSL))
        self.blk.gpsimd(_g2)
        def _v(v):
            pass
        self.blk.vector(_v)
        def _l(sy):
            self.vv.wait(sy)
            self.ld.dma(sy.dma_start(out=self.X[0:16, 0:1024], in_=self.WIN["D_mat"]))
        self.blk.sync(_l)
        def _v(v):
            self.ld.wait(v)
            v.tensor_copy(self.DMH, self.X[0:16, 0:1024])
            v.tensor_sub(self.DML, self.X[0:16, 0:1024], self.DMH)
            self.vv.inc(v.drain())
        self.blk.vector(_v)

    def emit_ys_outproj(self):
        ld = self.ld; vv = self.vv; sc = self.sc; mmS = self.mmS
        X = self.X; BFA = self.BFA; BFB = self.BFB; T2 = self.T2
        wh, wl = self.WS["out_w"]
        for r in range(4):
            half, b = divmod(r, 2)
            base_ld = ld.n; base_vv = vv.n; base_mm = mmS.n
            def _l(sy, r=r):
                vv.wait(sy)
                for cch in range(DC):
                    ld.dma(sy.dma_start(
                        out=X[:, self.ZSBs + cch * 512: self.ZSBs + (cch + 1) * 512],
                        in_=self.z_spill.ap()[:, cch * 2048 + r * 512: cch * 2048 + (r + 1) * 512]))
            self.blk.sync(_l)
            def _t(te, r=r):
                te.wait_ge(ld.h, base_ld + 16 * DC)
                te.wait_ge(vv.h, base_vv)
                hh = self.HSHI[:, r * 512:(r + 1) * 512]
                hl = self.HSLO[:, r * 512:(r + 1) * 512]
                for c in range(DC):
                    te.matmul(self.PS[c][:, :], self.DMH[:, c * 128:(c + 1) * 128], hh, start=True, stop=False)
                    te.matmul(self.PS[c][:, :], self.DML[:, c * 128:(c + 1) * 128], hh, start=False, stop=False)
                    bi = te.matmul(self.PS[c][:, :], self.DMH[:, c * 128:(c + 1) * 128], hl, start=False, stop=True)
                mmS.inc(bi)
            self.blk.tensor(_t)
            def _v(v, r=r):
                v.wait_ge(mmS.h, base_mm + 1)
                v.wait_ge(ld.h, base_ld + 16 * DC)
                for c in range(DC):
                    v.tensor_mul(X[:, self.TMPT + c * 512: self.TMPT + (c + 1) * 512],
                                 self.PS[c][:, :], X[:, self.ZSBs + c * 512: self.ZSBs + (c + 1) * 512])
                v.tensor_copy(BFA[:, 0:4096], X[:, self.TMPT: self.TMPT + 4096])
                v.tensor_sub(BFB[:, 0:4096], X[:, self.TMPT: self.TMPT + 4096], BFA[:, 0:4096])
                self.vv.inc(v.drain())
            self.blk.vector(_v)
            base_ld3 = ld.n
            def _lw(sy, r=r):
                for k in range(DC):
                    if k < 2:
                        sy.wait_ge(mmS.h, base_mm + 1)
                    else:
                        sy.wait_ge(mmS.h, base_mm + 1 + (k - 1))
                    ld.dma(sy.dma_start(out=self.WSL[2 * (k % 2)][:, :], in_=wh.ap()[k * 128:(k + 1) * 128, :]))
                    ld.dma(sy.dma_start(out=self.WSL[2 * (k % 2) + 1][:, :], in_=wl.ap()[k * 128:(k + 1) * 128, :]))
            self.blk.sync(_lw)
            def _t(te, r=r):
                te.wait_ge(vv.h, base_vv + 1)
                for k in range(DC):
                    te.wait_ge(ld.h, base_ld3 + 32 * (k + 1))
                    whs = self.WSL[2 * (k % 2)]; wls = self.WSL[2 * (k % 2) + 1]
                    yh = BFA[:, k * 512:(k + 1) * 512]
                    yl = BFB[:, k * 512:(k + 1) * 512]
                    for fc in range(8):
                        te.matmul(self.PS[fc][:, :], whs[:, fc * 128:(fc + 1) * 128], yh, start=(k == 0), stop=False)
                        te.matmul(self.PS[fc][:, :], wls[:, fc * 128:(fc + 1) * 128], yh, start=False, stop=False)
                        bi = te.matmul(self.PS[fc][:, :], whs[:, fc * 128:(fc + 1) * 128], yl, start=False,
                                       stop=(k == DC - 1))
                    mmS.inc(bi)
            self.blk.tensor(_t)
            def _v(v, r=r, half=half, b=b):
                mmS.wait(v)
                for fc in range(8):
                    dstap = T2[:, fc * 2048 + half * 1024: fc * 2048 + half * 1024 + 1024].rearrange(
                        "p (w two) -> p two w", two=2)[:, b, :]
                    v.scalar_tensor_tensor(dstap, self.PS[fc][:, :], self.PRM["out_b"][:, fc:fc + 1],
                                                  dstap, ALU.add, ALU.add)
                    self.vv.inc(v.drain())
            self.blk.vector(_v)

    def emit_ln2_mlp(self):
        ld = self.ld; vv = self.vv; sc = self.sc; mmS = self.mmS
        X = self.X; T2 = self.T2; BFA = self.BFA; BFB = self.BFB; FA = self.FA
        base_mm = mmS.n; base_sc = sc.n; base_vv = vv.n
        def _s(se):
            vv.wait(se)
            for g in range(4):
                for c in range(DC):
                    if g * DC + c >= 4:
                        se.wait_ge(mmS.h, base_mm + (g * DC + c - 3))
                    se.activation(BFA[:, (c % 4) * 512:(c % 4) * 512 + 512],
                                         T2[:, c * 2048 + g * 512: c * 2048 + (g + 1) * 512], ACT.Square)
                    self.sc.inc(se.drain())
        self.blk.scalar(_s)
        def _t(te):
            for g in range(4):
                px, pq = self.PS[2 * (g % 2)], self.PS[2 * (g % 2) + 1]
                if g >= 2:
                    te.wait_ge(vv.h, base_vv + (g - 1))
                for c in range(DC):
                    te.wait_ge(sc.h, base_sc + min(g * DC + c + 2, 4 * DC))
                    te.matmul(px[0:1, :], self.ONESF[:, 0:1], T2[:, c * 2048 + g * 512: c * 2048 + (g + 1) * 512],
                              start=(c == 0), stop=(c == DC - 1))
                    bi = te.matmul(pq[0:1, :], self.ONESB[:, 0:1], BFA[:, (c % 4) * 512:(c % 4) * 512 + 512],
                                   start=(c == 0), stop=(c == DC - 1))
                    mmS.inc(bi)
        self.blk.tensor(_t)
        m3 = self.SROW[0:1, 0:2048]
        v3 = X[0:1, 0:2048]
        r3 = X[0:1, 2080:2080 + 2048]
        def _v(v):
            for g in range(4):
                px, pq = self.PS[2 * (g % 2)], self.PS[2 * (g % 2) + 1]
                v.wait_ge(mmS.h, base_mm + DC * (g + 1))
                v.tensor_copy(self.SROW[0:1, g * 512:(g + 1) * 512], px[0:1, :])
                v.tensor_copy(v3[0:1, g * 512:(g + 1) * 512], pq[0:1, :])
                self.vv.inc(v.drain())
        self.blk.vector(_v)
        def _v(v):
            vv.wait(v)
            v.tensor_scalar_mul(m3, m3, 1.0 / 1024.0)
            v.tensor_scalar_mul(v3, v3, 1.0 / 1024.0)
            v.tensor_mul(r3, m3, m3)
            v.tensor_sub(v3, v3, r3)
            self.vv.inc(v.drain())
        self.blk.vector(_v)
        def _s(se):
            vv.wait(se)
            se.activation(r3, v3, ACT.Sqrt, bias=self.EPS[0:1, 0:1])
            self.sc.inc(se.drain())
        self.blk.scalar(_s)
        def _v(v):
            sc.wait(v)
            v.reciprocal(v3, r3)
            self.vv.inc(v.drain())
        self.blk.vector(_v)
        def _b(sy):
            vv.wait(sy)
            ld.dma(sy.dma_start(out=self.row_scr.ap()[5:6, 0:2048], in_=m3))
            ld.dma(sy.dma_start(out=self.row_scr.ap()[6:7, 0:2048], in_=v3))
            sy.wait_ge(ld.h, ld.n)
        self.blk.sync(_b)
        w1h, w1l = self.WS["mlp_w1"]
        w2h, w2l = self.WS["mlp_w2"]
        MHoff = 0
        BCMoff = 4096
        BCRoff = 4352
        for ts in range(8):
            half = ts // 4
            j0 = (ts % 4) * 256
            tok0 = half * 1024 + j0
            base_ld = ld.n; base_vv = vv.n; base_sc = sc.n; base_mm = mmS.n
            def _l(sy, ts=ts, half=half, j0=j0, tok0=tok0):
                vv.wait(sy); sc.wait(sy)
                ld.dma(sy.dma_start(out=FA[:, BCMoff:BCMoff + 256],
                                    in_=bass.AP(self.row_scr, 5 * 2080 + tok0, [[0, 128], [1, 256]])))
                ld.dma(sy.dma_start(out=FA[:, BCRoff:BCRoff + 256],
                                    in_=bass.AP(self.row_scr, 6 * 2080 + tok0, [[0, 128], [1, 256]])))
                for c in range(DC):
                    ld.dma(sy.dma_start(out=X[:, c * 2048 + tok0: c * 2048 + tok0 + 256],
                                        in_=self.xs[c * 128:(c + 1) * 128,
                                                    half * SW + 3 + j0: half * SW + 3 + j0 + 256]))
            self.blk.sync(_l)
            def _v(v, ts=ts, tok0=tok0):
                v.wait_ge(ld.h, base_ld + 16 * (2 + DC))
                for c in range(DC):
                    v.tensor_add(X[:, c * 2048 + tok0: c * 2048 + tok0 + 256],
                                 X[:, c * 2048 + tok0: c * 2048 + tok0 + 256],
                                 T2[:, c * 2048 + tok0: c * 2048 + tok0 + 256])
                last = None
                for c in range(DC):
                    xh = FA[:, BCMoff:BCMoff + 256]
                    xh = FA[:, 3840:3840 + 256]
                    v.tensor_sub(xh, T2[:, c * 2048 + tok0: c * 2048 + tok0 + 256], FA[:, BCMoff:BCMoff + 256])
                    v.tensor_mul(xh, xh, FA[:, BCRoff:BCRoff + 256])
                    v.tensor_scalar(xh, xh, self.PRM["ln2_w"][:, c:c + 1], self.PRM["ln2_b"][:, c:c + 1],
                                    ALU.mult, ALU.add)
                    v.tensor_copy(BFA[:, 8192 + c * 256: 8192 + (c + 1) * 256], xh)
                    last = v.tensor_sub(BFB[:, 4096 + c * 256: 4096 + (c + 1) * 256], xh,
                                        BFA[:, 8192 + c * 256: 8192 + (c + 1) * 256])
                vv.inc(last)
            self.blk.vector(_v)
            for fcg in range(2):
                base_ld2 = ld.n
                def _lw(sy, fcg=fcg):
                    for k in range(DC):
                        if k < 2:
                            sy.wait_ge(mmS.h, base_mm + fcg * DC)
                        else:
                            sy.wait_ge(mmS.h, base_mm + fcg * DC + (k - 1))
                        ld.dma(sy.dma_start(out=self.WSL[2 * (k % 2)][:, :],
                                            in_=w1h.ap()[k * 128:(k + 1) * 128, fcg * 1024:(fcg + 1) * 1024]))
                        ld.dma(sy.dma_start(out=self.WSL[2 * (k % 2) + 1][:, :],
                                            in_=w1l.ap()[k * 128:(k + 1) * 128, fcg * 1024:(fcg + 1) * 1024]))
                self.blk.sync(_lw)
                def _t(te, fcg=fcg):
                    te.wait_ge(vv.h, base_vv + 1)
                    if fcg == 1:
                        te.wait_ge(sc.h, base_sc + 8)
                    for k in range(DC):
                        te.wait_ge(ld.h, base_ld2 + 32 * (k + 1))
                        whs = self.WSL[2 * (k % 2)]; wls = self.WSL[2 * (k % 2) + 1]
                        nh = BFA[:, 8192 + k * 256: 8192 + (k + 1) * 256]
                        nl = BFB[:, 4096 + k * 256: 4096 + (k + 1) * 256]
                        for fc in range(8):
                            psb = self.PS[fc]
                            te.matmul(psb[:, 0:256], whs[:, fc * 128:(fc + 1) * 128], nh, start=(k == 0), stop=False)
                            te.matmul(psb[:, 0:256], wls[:, fc * 128:(fc + 1) * 128], nh, start=False, stop=False)
                            bi = te.matmul(psb[:, 0:256], whs[:, fc * 128:(fc + 1) * 128], nl, start=False,
                                           stop=(k == DC - 1))
                        mmS.inc(bi)
                self.blk.tensor(_t)
                def _s(se, fcg=fcg):
                    se.wait_ge(mmS.h, base_mm + (fcg + 1) * DC)
                    for fc in range(8):
                        fcol = fcg * 8 + fc
                        se.activation(FA[:, MHoff + fcol * 256: MHoff + (fcol + 1) * 256],
                                             self.PS[fc][:, 0:256],
                                             ACT.Gelu, bias=self.PRM["mlp_b1"][:, fcol:fcol + 1])
                        self.sc.inc(se.drain())
                self.blk.scalar(_s)
            def _v(v, ts=ts):
                v.wait_ge(sc.h, base_sc + 16)
                v.tensor_copy(BFA[:, 0:4096], FA[:, MHoff:MHoff + 4096])
                v.tensor_sub(BFB[:, 0:4096], FA[:, MHoff:MHoff + 4096], BFA[:, 0:4096])
                self.vv.inc(v.drain())
            self.blk.vector(_v)
            base_ld3 = ld.n; base_mm3 = mmS.n
            def _lw2(sy, ts=ts):
                for k in range(16):
                    if k < 2:
                        sy.wait_ge(mmS.h, base_mm3)
                    else:
                        sy.wait_ge(mmS.h, base_mm3 + (k - 1))
                    ld.dma(sy.dma_start(out=self.WSL[2 * (k % 2)][:, :], in_=w2h.ap()[k * 128:(k + 1) * 128, :]))
                    ld.dma(sy.dma_start(out=self.WSL[2 * (k % 2) + 1][:, :], in_=w2l.ap()[k * 128:(k + 1) * 128, :]))
            self.blk.sync(_lw2)
            def _t(te, ts=ts):
                te.wait_ge(vv.h, base_vv + 2)
                for k in range(16):
                    te.wait_ge(ld.h, base_ld3 + 32 * (k + 1))
                    whs = self.WSL[2 * (k % 2)]; wls = self.WSL[2 * (k % 2) + 1]
                    nh = BFA[:, k * 256:(k + 1) * 256]
                    nl = BFB[:, k * 256:(k + 1) * 256]
                    for fc in range(8):
                        psb = self.PS[fc]
                        te.matmul(psb[:, 0:256], whs[:, fc * 128:(fc + 1) * 128], nh, start=(k == 0), stop=False)
                        te.matmul(psb[:, 0:256], wls[:, fc * 128:(fc + 1) * 128], nh, start=False, stop=False)
                        bi = te.matmul(psb[:, 0:256], whs[:, fc * 128:(fc + 1) * 128], nl, start=False,
                                       stop=(k == 15))
                    mmS.inc(bi)
            self.blk.tensor(_t)
            def _v(v, ts=ts, tok0=tok0):
                mmS.wait(v)
                for fc in range(8):
                    dst = X[:, fc * 2048 + tok0: fc * 2048 + tok0 + 256]
                    v.scalar_tensor_tensor(dst, self.PS[fc][:, 0:256], self.PRM["mlp_b2"][:, fc:fc + 1],
                                                  dst, ALU.add, ALU.add)
                    self.vv.inc(v.drain())
            self.blk.vector(_v)

    def emit_stats_rows(self, SRC, v3row, r3row, rm, rr):
        ld = self.ld; vv = self.vv; sc = self.sc; mmS = self.mmS
        BFA = self.BFA
        base_mm = mmS.n; base_sc = sc.n; base_vv = vv.n
        def _s(se):
            vv.wait(se)
            for g in range(4):
                for c in range(DC):
                    if g * DC + c >= 4:
                        se.wait_ge(mmS.h, base_mm + (g * DC + c - 3))
                    elif g == 0 and c == 0:
                        se.wait_ge(mmS.h, base_mm)
                    se.activation(BFA[:, (c % 4) * 512:(c % 4) * 512 + 512],
                                  SRC[:, c * 2048 + g * 512: c * 2048 + (g + 1) * 512], ACT.Square)
                    self.sc.inc(se.drain())
        self.blk.scalar(_s)
        def _t(te):
            for g in range(4):
                px, pq = self.PS[2 * (g % 2)], self.PS[2 * (g % 2) + 1]
                if g >= 2:
                    te.wait_ge(vv.h, base_vv + (g - 1))
                for c in range(DC):
                    te.wait_ge(sc.h, base_sc + min(g * DC + c + 2, 4 * DC))
                    te.matmul(px[0:1, :], self.ONESF[:, 0:1],
                              SRC[:, c * 2048 + g * 512: c * 2048 + (g + 1) * 512],
                              start=(c == 0), stop=(c == DC - 1))
                    bi = te.matmul(pq[0:1, :], self.ONESB[:, 0:1], BFA[:, (c % 4) * 512:(c % 4) * 512 + 512],
                                   start=(c == 0), stop=(c == DC - 1))
                    self.mmS.inc(bi)
        self.blk.tensor(_t)
        m3 = self.SROW[0:1, 0:2048]
        def _v(v):
            for g in range(4):
                px, pq = self.PS[2 * (g % 2)], self.PS[2 * (g % 2) + 1]
                v.wait_ge(mmS.h, base_mm + DC * (g + 1))
                v.tensor_copy(m3[0:1, g * 512:(g + 1) * 512], px[0:1, :])
                v.tensor_copy(v3row[0:1, g * 512:(g + 1) * 512], pq[0:1, :])
                self.vv.inc(v.drain())
            v.tensor_scalar_mul(m3, m3, 1.0 / 1024.0)
            v.tensor_scalar_mul(v3row, v3row, 1.0 / 1024.0)
            v.tensor_mul(r3row, m3, m3)
            v.tensor_sub(v3row, v3row, r3row)
            self.vv.inc(v.drain())
        self.blk.vector(_v)
        def _s2(se):
            vv.wait(se)
            se.activation(r3row, v3row, ACT.Sqrt, bias=self.EPS[0:1, 0:1])
            self.sc.inc(se.drain())
        self.blk.scalar(_s2)
        def _v2(v):
            sc.wait(v)
            v.reciprocal(v3row, r3row)
            self.vv.inc(v.drain())
        self.blk.vector(_v2)
        def _b(sy):
            vv.wait(sy)
            ld.dma(sy.dma_start(out=self.row_scr.ap()[rm:rm + 1, 0:2048], in_=m3))
            ld.dma(sy.dma_start(out=self.row_scr.ap()[rr:rr + 1, 0:2048], in_=v3row))
            sy.wait_ge(ld.h, ld.n)
        self.blk.sync(_b)

    def emit_phaseB(self):
        ld = self.ld; vv = self.vv; sc = self.sc; mmS = self.mmS
        X = self.X; T2 = self.T2; BFA = self.BFA; BFB = self.BFB
        self.emit_stats_rows(X, T2[0:1, 0:2048], T2[0:1, 2080:2080 + 2048], 5, 6)
        HB, BCM4, BCR4, XQT = 0, 11264, 11776, 12288
        wgh, wgl = self.WS["wgT"]; wuh, wul = self.WS["wuT"]; wdh, wdl = self.WS["wdT"]
        gd = self.gd
        dbgB = bool({"bxh", "bxq", "bh", "bhq", "btm", "bg"} & self.dbg)
        if dbgB:
            self.bxh_sp = self.dscr("bxh_sp", [128, 16384])
            self.bxq_sp = self.dscr("bxq_sp", [128, 16384])
            self.bh_sp = self.dscr("bh_sp", [128, 45056])
            self.bhq_sp = self.dscr("bhq_sp", [128, 45056])
        for ts in range(4):
            tok0 = ts * 512
            base_ld = ld.n
            def _l(sy, tok0=tok0):
                vv.wait(sy); sc.wait(sy)
                if dbgB:
                    sy.wait_ge(gd.h, gd.n)
                ld.dma(sy.dma_start(out=T2[:, BCM4:BCM4 + 512],
                                    in_=bass.AP(self.row_scr, 5 * 2080 + tok0, [[0, 128], [1, 512]])))
                ld.dma(sy.dma_start(out=T2[:, BCR4:BCR4 + 512],
                                    in_=bass.AP(self.row_scr, 6 * 2080 + tok0, [[0, 128], [1, 512]])))
            self.blk.sync(_l)
            def _v(v, tok0=tok0, base_ld=base_ld):
                v.wait_ge(ld.h, base_ld + 32)
                for c in range(DC):
                    xh = T2[:, XQT + c * 512: XQT + (c + 1) * 512]
                    v.tensor_sub(xh, X[:, c * 2048 + tok0: c * 2048 + tok0 + 512], T2[:, BCM4:BCM4 + 512])
                    v.tensor_mul(xh, xh, T2[:, BCR4:BCR4 + 512])
                self.vv.inc(v.drain())
            self.blk.vector(_v)
            gd_bxh = None
            if dbgB:
                def _gx(g, ts=ts):
                    vv.wait(g)
                    gd.dma(g.dma_start(out=self.bxh_sp.ap()[:, ts * 4096:(ts + 1) * 4096],
                                       in_=T2[:, XQT: XQT + 4096]))
                self.blk.gpsimd(_gx)
                gd_bxh = gd.n
            blocks = [(T2[:, XQT + c * 512 + tt * 128: XQT + c * 512 + (tt + 1) * 128],
                       T2[:, HB + tt * 1024 + c * 128: HB + tt * 1024 + (c + 1) * 128])
                      for tt in range(4) for c in range(DC)]
            self.emit_transposes(blocks, self.IDN)
            gd_btm = None
            if "btm" in self.dbg:
                if not hasattr(self, "btm_sp"):
                    self.btm_sp = self.dscr("btm_sp", [128, 16384])
                def _gtm(g, ts=ts):
                    vv.wait(g)
                    gd.dma(g.dma_start(out=self.btm_sp.ap()[:, ts * 4096:(ts + 1) * 4096],
                                       in_=T2[:, HB: HB + 4096]))
                self.blk.gpsimd(_gtm)
                gd_btm = gd.n
            def _vq(v, gd_btm=gd_btm):
                if gd_btm is not None:
                    v.wait_ge(gd.h, gd_btm)
                for piece in range(4):
                    srcp = T2[:, HB + piece * 1024: HB + (piece + 1) * 1024]
                    self.emit_quant(v, srcp, srcp, 1024, 0)
                    v.drain()
                self.vv.inc(v.drain())
            self.blk.vector(_vq)
            blocks = [(T2[:, HB + tt * 1024 + c * 128: HB + tt * 1024 + (c + 1) * 128],
                       T2[:, XQT + c * 512 + tt * 128: XQT + c * 512 + (tt + 1) * 128])
                      for tt in range(4) for c in range(DC)]
            self.emit_transposes(blocks, self.IDN,
                                 extra_v_wait=(lambda v, n=gd_bxh: v.wait_ge(gd.h, n)) if dbgB else None)
            def _vs(v):
                v.tensor_copy(BFB[:, 0:4096], T2[:, XQT: XQT + 4096])
                v.tensor_sub(BFB[:, 4096:8192], T2[:, XQT: XQT + 4096], BFB[:, 0:4096])
                self.vv.inc(v.drain())
            self.blk.vector(_vs)
            gd_bxq = None
            if dbgB:
                def _gx2(g, ts=ts):
                    vv.wait(g)
                    gd.dma(g.dma_start(out=self.bxq_sp.ap()[:, ts * 4096:(ts + 1) * 4096],
                                       in_=T2[:, XQT: XQT + 4096]))
                self.blk.gpsimd(_gx2)
                gd_bxq = gd.n
            fgrps = [(0, 8), (8, 16), (16, 22)]
            gd_bg = None
            for mat in range(2):
                if mat == 1 and "bg" in self.dbg:
                    if not hasattr(self, "bg_sp"):
                        self.bg_sp = self.dscr("bg_sp", [128, 45056])
                    def _gbg(g, ts=ts):
                        vv.wait(g); sc.wait(g)
                        gd.dma(g.dma_start(out=self.bg_sp.ap()[:, ts * 11264:(ts + 1) * 11264],
                                           in_=T2[:, 0:11264]))
                    self.blk.gpsimd(_gbg)
                    gd_bg = gd.n
                for grp, (f0, f1) in enumerate(fgrps):
                    nfc = f1 - f0
                    wh2, wl2 = (wgh, wgl) if mat == 0 else (wuh, wul)
                    base_ld4 = ld.n; base_mm4 = mmS.n
                    def _lw(sy, f0=f0, f1=f1, wh2=wh2, wl2=wl2, base_mm4=base_mm4):
                        for k in range(DC):
                            if k < 2:
                                sy.wait_ge(mmS.h, base_mm4)
                            else:
                                sy.wait_ge(mmS.h, base_mm4 + (k - 1))
                            for fb in range(f0, f1):
                                rr0 = (fb * 8 + k) * 128
                                co = (fb - f0) * 128
                                ld.dma(sy.dma_start(out=self.WSL[2 * (k % 2)][:, co:co + 128],
                                                    in_=wh2.ap()[rr0:rr0 + 128, :]))
                                ld.dma(sy.dma_start(out=self.WSL[2 * (k % 2) + 1][:, co:co + 128],
                                                    in_=wl2.ap()[rr0:rr0 + 128, :]))
                    self.blk.sync(_lw)
                    base_vs = vv.n; base_ss = sc.n
                    def _t(te, nfc=nfc, base_ld4=base_ld4, base_vs=base_vs, base_ss=base_ss):
                        te.wait_ge(vv.h, base_vs)
                        te.wait_ge(sc.h, base_ss)
                        for k in range(DC):
                            te.wait_ge(ld.h, base_ld4 + 32 * nfc * (k + 1))
                            whs = self.WSL[2 * (k % 2)]; wls = self.WSL[2 * (k % 2) + 1]
                            xh = BFB[:, k * 512:(k + 1) * 512]
                            xl = BFB[:, 4096 + k * 512: 4096 + (k + 1) * 512]
                            for fc in range(nfc):
                                psb = self.PS[fc][:, :]
                                te.matmul(psb, whs[:, fc * 128:(fc + 1) * 128], xh, start=(k == 0), stop=False)
                                te.matmul(psb, wls[:, fc * 128:(fc + 1) * 128], xh, start=False, stop=False)
                                bi = te.matmul(psb, whs[:, fc * 128:(fc + 1) * 128], xl, start=False,
                                               stop=(k == DC - 1))
                            self.mmS.inc(bi)
                    self.blk.tensor(_t)
                    if mat == 0:
                        def _se(se, f0=f0, nfc=nfc, base_mm4=base_mm4):
                            se.wait_ge(mmS.h, base_mm4 + DC)
                            for fc in range(nfc):
                                se.activation(T2[:, (f0 + fc) * 512:(f0 + fc + 1) * 512],
                                              self.PS[fc][:, :], ACT.Silu)
                                self.sc.inc(se.drain())
                        self.blk.scalar(_se)
                    else:
                        base_ss2 = sc.n
                        def _ve(v, f0=f0, nfc=nfc, base_mm4=base_mm4, base_ss2=base_ss2, gd_bg=gd_bg):
                            v.wait_ge(mmS.h, base_mm4 + DC)
                            v.wait_ge(sc.h, base_ss2)
                            if gd_bg is not None:
                                v.wait_ge(gd.h, gd_bg)
                            for fc in range(nfc):
                                v.tensor_mul(T2[:, (f0 + fc) * 512:(f0 + fc + 1) * 512],
                                             self.PS[fc][:, :], T2[:, (f0 + fc) * 512:(f0 + fc + 1) * 512])
                            self.vv.inc(v.drain())
                        self.blk.vector(_ve)
            gd_bh = None
            if dbgB:
                def _gh(g, ts=ts):
                    vv.wait(g)
                    gd.dma(g.dma_start(out=self.bh_sp.ap()[:, ts * 11264:(ts + 1) * 11264],
                                       in_=T2[:, 0:11264]))
                self.blk.gpsimd(_gh)
                gd_bh = gd.n
            for qt in range(4):
                blocks = [(T2[:, c2 * 512 + qt * 128: c2 * 512 + qt * 128 + 128],
                           T2[:, XQT + c2 * 128: XQT + (c2 + 1) * 128]) for c2 in range(22)]
                self.emit_transposes(blocks, self.IDN,
                                     extra_v_wait=(lambda v, n=gd_bxq: v.wait_ge(gd.h, n))
                                     if (dbgB and qt == 0) else None)
                def _vhq(v):
                    for p0 in (0, 1024, 2048):
                        pw = min(1024, 2816 - p0)
                        self.emit_quant(v, T2[:, XQT + p0: XQT + p0 + pw], T2[:, XQT + p0: XQT + p0 + pw],
                                        pw, 0)
                        v.drain()
                    self.vv.inc(v.drain())
                self.blk.vector(_vhq)
                blocks = [(T2[:, XQT + c2 * 128: XQT + (c2 + 1) * 128],
                           T2[:, c2 * 512 + qt * 128: c2 * 512 + qt * 128 + 128]) for c2 in range(22)]
                self.emit_transposes(blocks, self.IDN,
                                     extra_v_wait=(lambda v, n=gd_bh: v.wait_ge(gd.h, n))
                                     if (dbgB and qt == 0) else None)
            if dbgB:
                def _ghq(g, ts=ts):
                    vv.wait(g)
                    gd.dma(g.dma_start(out=self.bhq_sp.ap()[:, ts * 11264:(ts + 1) * 11264],
                                       in_=T2[:, 0:11264]))
                self.blk.gpsimd(_ghq)
            base_mm7 = mmS.n
            for half, (k0, k1) in enumerate([(0, 11), (11, 22)]):
                base_vh = vv.n
                def _vh(v, k0=k0, k1=k1, half=half, base_mm7=base_mm7):
                    if half == 1:
                        v.wait_ge(mmS.h, base_mm7 + 11)
                    w = (k1 - k0) * 512
                    v.tensor_copy(BFA[:, 0:w], T2[:, k0 * 512: k1 * 512])
                    v.tensor_sub(BFB[:, 0:w], T2[:, k0 * 512: k1 * 512], BFA[:, 0:w])
                    self.vv.inc(v.drain())
                self.blk.vector(_vh)
                base_ld7 = ld.n
                def _lw(sy, k0=k0, k1=k1, base_mm7=base_mm7, half=half):
                    for jj, k2 in enumerate(range(k0, k1)):
                        if jj < 2:
                            sy.wait_ge(mmS.h, base_mm7 + half * 11)
                        else:
                            sy.wait_ge(mmS.h, base_mm7 + half * 11 + (jj - 1))
                        for db in range(8):
                            rr0 = (k2 * 8 + db) * 128
                            ld.dma(sy.dma_start(out=self.WSL[2 * (jj % 2)][:, db * 128:(db + 1) * 128],
                                                in_=wdh.ap()[rr0:rr0 + 128, :]))
                            ld.dma(sy.dma_start(out=self.WSL[2 * (jj % 2) + 1][:, db * 128:(db + 1) * 128],
                                                in_=wdl.ap()[rr0:rr0 + 128, :]))
                self.blk.sync(_lw)
                def _t(te, k0=k0, k1=k1, base_ld7=base_ld7, base_vh=base_vh, half=half):
                    te.wait_ge(vv.h, base_vh + 1)
                    for jj in range(k1 - k0):
                        te.wait_ge(ld.h, base_ld7 + 256 * (jj + 1))
                        whs = self.WSL[2 * (jj % 2)]; wls = self.WSL[2 * (jj % 2) + 1]
                        hh = BFA[:, jj * 512:(jj + 1) * 512]
                        hl = BFB[:, jj * 512:(jj + 1) * 512]
                        for fc in range(8):
                            psb = self.PS[fc][:, :]
                            te.matmul(psb, whs[:, fc * 128:(fc + 1) * 128], hh,
                                      start=(half == 0 and jj == 0), stop=False)
                            te.matmul(psb, wls[:, fc * 128:(fc + 1) * 128], hh, start=False, stop=False)
                            bi = te.matmul(psb, whs[:, fc * 128:(fc + 1) * 128], hl, start=False,
                                           stop=(half == 1 and jj == k1 - k0 - 1))
                        self.mmS.inc(bi)
                self.blk.tensor(_t)
            def _vx(v, tok0=tok0):
                v.wait_ge(mmS.h, mmS.n)
                for fc in range(8):
                    dst = X[:, fc * 2048 + tok0: fc * 2048 + tok0 + 512]
                    v.tensor_add(dst, dst, self.PS[fc][:, :])
                self.vv.inc(v.drain())
            self.blk.vector(_vx)
        if dbgB:
            self.dbg_emit("bxh", self.bxh_sp.ap(), [128, 16384])
            self.dbg_emit("bxq", self.bxq_sp.ap(), [128, 16384])
            self.dbg_emit("bh", self.bh_sp.ap(), [128, 45056])
            self.dbg_emit("bhq", self.bhq_sp.ap(), [128, 45056])
        if "btm" in self.dbg:
            self.dbg_emit("btm", self.btm_sp.ap(), [128, 16384])
        if "bg" in self.dbg:
            self.dbg_emit("bg", self.bg_sp.ap(), [128, 45056])

    def emit_phaseC(self):
        ld = self.ld; vv = self.vv; sc = self.sc; mmS = self.mmS
        X = self.X; T2 = self.T2; BFA = self.BFA; BFB = self.BFB; FA = self.FA
        self.emit_stats_rows(X, T2[0:1, 0:2048], T2[0:1, 2080:2080 + 2048], 5, 6)
        KV, S1o, S2o, SM, O0, O1, BCM = 0, 6144, 8192, 10240, 12288, 13312, 14336
        k0a, v0a, q0a = KV, KV + 1024, KV + 2048
        k1a, v1a, q1a = KV + 3072, KV + 4096, KV + 5120
        WT = [self.WS["WkT"], self.WS["WvT"], self.WS["WqT"]]
        for pr in range(8):
            base_ld = ld.n
            def _l(sy, pr=pr):
                vv.wait(sy)
                for tile in range(2):
                    tok0 = tile * 1024 + pr * 128
                    ld.dma(sy.dma_start(out=T2[:, BCM + tile * 256: BCM + tile * 256 + 128],
                                        in_=bass.AP(self.row_scr, 5 * 2080 + tok0, [[0, 128], [1, 128]])))
                    ld.dma(sy.dma_start(out=T2[:, BCM + tile * 256 + 128: BCM + tile * 256 + 256],
                                        in_=bass.AP(self.row_scr, 6 * 2080 + tok0, [[0, 128], [1, 128]])))
            self.blk.sync(_l)
            def _v(v, pr=pr, base_ld=base_ld):
                v.wait_ge(ld.h, base_ld + 64)
                for tile in range(2):
                    tok0 = tile * 1024 + pr * 128
                    for c in range(DC):
                        xh = FA[:, 0:128]
                        v.tensor_sub(xh, X[:, c * 2048 + tok0: c * 2048 + tok0 + 128],
                                     T2[:, BCM + tile * 256: BCM + tile * 256 + 128])
                        v.tensor_mul(xh, xh, T2[:, BCM + tile * 256 + 128: BCM + tile * 256 + 256])
                        v.tensor_copy(BFA[:, tile * 1024 + c * 128: tile * 1024 + (c + 1) * 128], xh)
                        v.tensor_sub(BFB[:, tile * 1024 + c * 128: tile * 1024 + (c + 1) * 128], xh,
                                     BFA[:, tile * 1024 + c * 128: tile * 1024 + (c + 1) * 128])
                self.vv.inc(v.drain())
            self.blk.vector(_v)
            for tile in range(2):
                for mat in range(3):
                    wh_, wl_ = WT[mat]
                    base_ld2 = ld.n; base_mm2 = mmS.n
                    def _lw(sy, wh_=wh_, wl_=wl_, base_mm2=base_mm2):
                        for k in range(DC):
                            if k < 2:
                                sy.wait_ge(mmS.h, base_mm2)
                            else:
                                sy.wait_ge(mmS.h, base_mm2 + (k - 1))
                            ld.dma(sy.dma_start(out=self.WSL[2 * (k % 2)][:, :],
                                                in_=wh_.ap()[k * 128:(k + 1) * 128, :]))
                            ld.dma(sy.dma_start(out=self.WSL[2 * (k % 2) + 1][:, :],
                                                in_=wl_.ap()[k * 128:(k + 1) * 128, :]))
                    self.blk.sync(_lw)
                    base_vv2 = vv.n
                    def _t(te, tile=tile, base_ld2=base_ld2, base_vv2=base_vv2):
                        te.wait_ge(vv.h, base_vv2)
                        for k in range(DC):
                            te.wait_ge(ld.h, base_ld2 + 32 * (k + 1))
                            whs = self.WSL[2 * (k % 2)]; wls = self.WSL[2 * (k % 2) + 1]
                            xh = BFA[:, tile * 1024 + k * 128: tile * 1024 + (k + 1) * 128]
                            xl = BFB[:, tile * 1024 + k * 128: tile * 1024 + (k + 1) * 128]
                            for hf in range(2):
                                psb = self.PS[hf][:, :]
                                te.matmul(psb, xh, whs[:, hf * 512:(hf + 1) * 512], start=(k == 0), stop=False)
                                te.matmul(psb, xl, whs[:, hf * 512:(hf + 1) * 512], start=False, stop=False)
                                bi = te.matmul(psb, xh, wls[:, hf * 512:(hf + 1) * 512], start=False,
                                               stop=(k == DC - 1))
                            self.mmS.inc(bi)
                    self.blk.tensor(_t)
                    def _ve(v, tile=tile, mat=mat, base_mm2=base_mm2):
                        v.wait_ge(mmS.h, base_mm2 + DC)
                        dst0 = KV + (tile * 3 + mat) * 1024
                        v.tensor_copy(T2[:, dst0: dst0 + 512], self.PS[0][:, :])
                        v.tensor_copy(T2[:, dst0 + 512: dst0 + 1024], self.PS[1][:, :])
                        self.vv.inc(v.drain())
                    self.blk.vector(_ve)
            for ng in range(4):
                nb0 = ng * 256
                def _v1(v, nb0=nb0):
                    sc.wait(v)
                    for st, ka in ((0, k0a), (1, k1a)):
                        ks = T2[:, ka + nb0: ka + nb0 + 256]
                        v.tensor_mul(T2[:, SM:SM + 256], ks, ks)
                        v.drain()
                        v.tensor_reduce(T2[:, SM + 256 + st * 32: SM + 288 + st * 32],
                                        T2[:, SM:SM + 256].rearrange("p (n l) -> p n l", l=8),
                                        AX.X, ALU.add)
                        v.drain()
                    self.vv.inc(v.drain())
                self.blk.vector(_v1)
                def _s1(se):
                    vv.wait(se)
                    se.activation(T2[:, SM + 320:SM + 384], T2[:, SM + 256:SM + 320], ACT.Sqrt)
                    self.sc.inc(se.drain())
                self.blk.scalar(_s1)
                def _v2(v, nb0=nb0):
                    sc.wait(v)
                    v.tensor_scalar_add(T2[:, SM + 320:SM + 384], T2[:, SM + 320:SM + 384], 1e-6)
                    v.drain()
                    v.reciprocal(T2[:, SM + 384:SM + 448], T2[:, SM + 320:SM + 384])
                    v.drain()
                    for st, ka, kno in ((0, k0a, SM + 448), (1, k1a, SM + 704)):
                        alb = T2[:, SM + 384 + st * 32: SM + 416 + st * 32].unsqueeze(2).broadcast_to([128, 32, 8])
                        v.tensor_mul(T2[:, kno:kno + 256].rearrange("p (n l) -> p n l", l=8),
                                     T2[:, ka + nb0:ka + nb0 + 256].rearrange("p (n l) -> p n l", l=8), alb)
                    v.drain()
                    kn0v = T2[:, SM + 448:SM + 704].rearrange("p (n l) -> p n l", l=8)
                    v0v = T2[:, v0a + nb0:v0a + nb0 + 256].rearrange("p (n i) -> p n i", i=8)
                    S1v = T2[:, S1o:S1o + 2048].rearrange("p (n i l) -> p n i l", i=8, l=8)
                    for i in range(8):
                        v.tensor_mul(S1v[:, :, i, :], kn0v, v0v[:, :, i:i + 1].broadcast_to([128, 32, 8]))
                    self.vv.inc(v.drain())
                self.blk.vector(_v2)
                def _s2(se):
                    vv.wait(se)
                    se.activation(T2[:, S2o:S2o + 2048], T2[:, S1o:S1o + 2048], ACT.Tanh)
                    self.sc.inc(se.drain())
                self.blk.scalar(_s2)
                def _v3(v, nb0=nb0):
                    sc.wait(v)
                    S1t = T2[:, S2o:S2o + 2048].rearrange("p (n i l) -> p n i l", i=8, l=8)
                    tmp = FA[:, 0:2048].rearrange("p (n i l) -> p n i l", i=8, l=8)
                    tmpf = FA[:, 0:2048].rearrange("p (ni l) -> p ni l", l=8)
                    q0v = T2[:, q0a + nb0:q0a + nb0 + 256].rearrange("p (n l) -> p n l", l=8)
                    kn1v = T2[:, SM + 704:SM + 960].rearrange("p (n l) -> p n l", l=8)
                    for i in range(8):
                        v.tensor_mul(tmp[:, :, i, :], S1t[:, :, i, :], q0v)
                    v.drain()
                    v.tensor_reduce(T2[:, O0 + nb0: O0 + nb0 + 256], tmpf, AX.X, ALU.add)
                    v.drain()
                    for i in range(8):
                        v.tensor_mul(tmp[:, :, i, :], S1t[:, :, i, :], kn1v)
                    v.drain()
                    v.tensor_reduce(T2[:, SM + 960:SM + 1216], tmpf, AX.X, ALU.add)
                    v.drain()
                    v.tensor_sub(T2[:, SM + 1216:SM + 1472], T2[:, v1a + nb0:v1a + nb0 + 256],
                                 T2[:, SM + 960:SM + 1216])
                    v.drain()
                    S2r = T2[:, S1o:S1o + 2048].rearrange("p (n i l) -> p n i l", i=8, l=8)
                    dl3 = T2[:, SM + 1216:SM + 1472].rearrange("p (n i) -> p n i", i=8)
                    for i in range(8):
                        v.tensor_mul(S2r[:, :, i, :], kn1v, dl3[:, :, i:i + 1].broadcast_to([128, 32, 8]))
                    v.drain()
                    v.tensor_add(T2[:, S1o:S1o + 2048], T2[:, S1o:S1o + 2048], T2[:, S2o:S2o + 2048])
                    self.vv.inc(v.drain())
                self.blk.vector(_v3)
                def _s3(se):
                    vv.wait(se)
                    se.activation(FA[:, 2048:4096], T2[:, S1o:S1o + 2048], ACT.Tanh)
                    self.sc.inc(se.drain())
                self.blk.scalar(_s3)
                def _v4(v, nb0=nb0):
                    sc.wait(v)
                    S2t = FA[:, 2048:4096].rearrange("p (n i l) -> p n i l", i=8, l=8)
                    tmp = FA[:, 0:2048].rearrange("p (n i l) -> p n i l", i=8, l=8)
                    tmpf = FA[:, 0:2048].rearrange("p (ni l) -> p ni l", l=8)
                    q1v = T2[:, q1a + nb0:q1a + nb0 + 256].rearrange("p (n l) -> p n l", l=8)
                    for i in range(8):
                        v.tensor_mul(tmp[:, :, i, :], S2t[:, :, i, :], q1v)
                    v.drain()
                    v.tensor_reduce(T2[:, O1 + nb0: O1 + nb0 + 256], tmpf, AX.X, ALU.add)
                    self.vv.inc(v.drain())
                self.blk.vector(_v4)
            def _so(se):
                vv.wait(se)
                se.activation(T2[:, SM:SM + 1024], T2[:, O0:O0 + 1024], ACT.Sigmoid)
                se.activation(T2[:, SM + 1024:SM + 2048], T2[:, O1:O1 + 1024], ACT.Sigmoid)
                self.sc.inc(se.drain())
            self.blk.scalar(_so)
            def _vo(v):
                sc.wait(v)
                v.tensor_mul(T2[:, O0:O0 + 1024], T2[:, O0:O0 + 1024], T2[:, O0:O0 + 1024])
                v.drain()
                v.tensor_mul(T2[:, O0:O0 + 1024], T2[:, O0:O0 + 1024], T2[:, SM:SM + 1024])
                v.tensor_mul(T2[:, O1:O1 + 1024], T2[:, O1:O1 + 1024], T2[:, O1:O1 + 1024])
                v.drain()
                v.tensor_mul(T2[:, O1:O1 + 1024], T2[:, O1:O1 + 1024], T2[:, SM + 1024:SM + 2048])
                self.vv.inc(v.drain())
            self.blk.vector(_vo)
            for tile, Oo in ((0, O0), (1, O1)):
                tok0 = tile * 1024 + pr * 128
                blocks = [(T2[:, Oo + c * 128: Oo + (c + 1) * 128],
                           X[:, c * 2048 + tok0: c * 2048 + tok0 + 128]) for c in range(DC)]
                self.emit_transposes(blocks, self.IDN, combine="add")

    def emit_phaseD(self):
        ld = self.ld; vv = self.vv; sc = self.sc; mmS = self.mmS; gd = self.gd
        X = self.X; T2 = self.T2; BFA = self.BFA; BFB = self.BFB; FA = self.FA
        self.emit_stats_rows(X, T2[0:1, 0:2048], T2[0:1, 2080:2080 + 2048], 5, 6)
        BCM6, BCR6 = 14336, 14848
        w1h, w1l = self.WS["w1e"]; w2h, w2l = self.WS["w2e"]
        sguh, sgul = self.WS["sgu"]; sdwh, sdwl = self.WS["sdw"]
        for ts in range(4):
            tok0 = ts * 512
            base_ld = ld.n; base_gd_pre = gd.n
            def _l(sy, tok0=tok0):
                vv.wait(sy); sc.wait(sy)
                ld.dma(sy.dma_start(out=T2[:, BCM6:BCM6 + 512],
                                    in_=bass.AP(self.row_scr, 5 * 2080 + tok0, [[0, 128], [1, 512]])))
                ld.dma(sy.dma_start(out=T2[:, BCR6:BCR6 + 512],
                                    in_=bass.AP(self.row_scr, 6 * 2080 + tok0, [[0, 128], [1, 512]])))
            self.blk.sync(_l)
            def _v(v, tok0=tok0, base_ld=base_ld, base_gd_pre=base_gd_pre):
                v.wait_ge(ld.h, base_ld + 32)
                v.wait_ge(gd.h, base_gd_pre)
                for c in range(DC):
                    xh = T2[:, c * 512:(c + 1) * 512]
                    v.tensor_sub(xh, X[:, c * 2048 + tok0: c * 2048 + tok0 + 512], T2[:, BCM6:BCM6 + 512])
                    v.tensor_mul(xh, xh, T2[:, BCR6:BCR6 + 512])
                    v.tensor_copy(BFA[:, c * 512:(c + 1) * 512], xh)
                    v.tensor_sub(BFB[:, c * 512:(c + 1) * 512], xh, BFA[:, c * 512:(c + 1) * 512])
                self.vv.inc(v.drain())
            self.blk.vector(_v)
            def _g(g, tok0=tok0):
                vv.wait(g)
                for c in range(DC):
                    gd.dma(g.dma_start(out=self.xh4_hi_sp.ap()[:, c * 2048 + tok0: c * 2048 + tok0 + 512],
                                       in_=BFA[:, c * 512:(c + 1) * 512]))
                    gd.dma(g.dma_start(out=self.xh4_lo_sp.ap()[:, c * 2048 + tok0: c * 2048 + tok0 + 512],
                                       in_=BFB[:, c * 512:(c + 1) * 512]))
            self.blk.gpsimd(_g)
            base_mmR = mmS.n; base_vvR = vv.n
            def _tr(te, base_mmR=base_mmR, base_vvR=base_vvR):
                te.wait_ge(vv.h, base_vvR)
                for c in range(DC):
                    xh = BFA[:, c * 512:(c + 1) * 512]
                    xl = BFB[:, c * 512:(c + 1) * 512]
                    te.matmul(self.PS[0][0:8, :], self.RWH[:, c * 8:(c + 1) * 8], xh,
                              start=(c == 0), stop=False)
                    te.matmul(self.PS[0][0:8, :], self.RWL[:, c * 8:(c + 1) * 8], xh,
                              start=False, stop=False)
                    bi = te.matmul(self.PS[0][0:8, :], self.RWH[:, c * 8:(c + 1) * 8], xl,
                                   start=False, stop=(c == DC - 1))
                self.mmS.inc(bi)
            self.blk.tensor(_tr)
            def _ve(v, ts=ts, base_mmR=base_mmR):
                v.wait_ge(mmS.h, base_mmR + 1)
                v.tensor_copy(FA[0:8, 2048 + ts * 512: 2048 + (ts + 1) * 512], self.PS[0][0:8, :])
                self.vv.inc(v.drain())
            self.blk.vector(_ve)
        base_mmT = mmS.n; base_vvT = vv.n
        def _tt(te):
            te.wait_ge(vv.h, base_vvT)
            for blk in range(16):
                if blk >= 4:
                    te.wait_ge(vv.h, base_vvT + (blk - 3))
                te.transpose(self.PS[4 + blk % 4][:, 0:128], FA[:, 2048 + blk * 128: 2048 + (blk + 1) * 128],
                             self.IDN[:, :])
                self.mmS.inc(te.drain())
        self.blk.tensor(_tt)
        def _tv(v):
            for blk in range(16):
                v.wait_ge(mmS.h, base_mmT + blk + 1)
                v.tensor_copy(FA[:, 4096 + blk * 8: 4096 + (blk + 1) * 8], self.PS[4 + blk % 4][:, 0:8])
                self.vv.inc(v.drain())
        self.blk.vector(_tv)
        def _sR(se):
            vv.wait(se)
            se.activation(FA[:, 4224:4352], FA[:, 4096:4224], ACT.Exp)
            self.sc.inc(se.drain())
        self.blk.scalar(_sR)
        def _vR(v):
            sc.wait(v)
            P3 = FA[:, 4224:4352].rearrange("p (n l) -> p n l", l=8)
            pr3 = FA[:, 4096:4224].rearrange("p (n l) -> p n l", l=8)
            msk = FA[:, 4416:4544].rearrange("p (n l) -> p n l", l=8)
            v.tensor_reduce(FA[:, 4352:4368], P3, AX.X, ALU.add)
            v.drain()
            v.reciprocal(FA[:, 4368:4384], FA[:, 4352:4368])
            v.drain()
            v.tensor_mul(pr3, P3, FA[:, 4368:4384].unsqueeze(2).broadcast_to([128, 16, 8]))
            v.drain()
            v.tensor_reduce(FA[:, 4384:4400], pr3, AX.X, ALU.max)
            v.drain()
            v.tensor_tensor(msk, pr3, FA[:, 4384:4400].unsqueeze(2).broadcast_to([128, 16, 8]), ALU.is_lt)
            v.drain()
            v.tensor_mul(FA[:, 4224:4352], FA[:, 4096:4224], FA[:, 4416:4544])
            v.drain()
            v.tensor_reduce(FA[:, 4400:4416], P3, AX.X, ALU.max)
            v.drain()
            v.tensor_tensor(msk, pr3, FA[:, 4400:4416].unsqueeze(2).broadcast_to([128, 16, 8]), ALU.is_ge)
            v.drain()
            v.tensor_mul(FA[:, 4224:4352], FA[:, 4096:4224], FA[:, 4416:4544])
            self.vv.inc(v.drain())
        self.blk.vector(_vR)
        self.dbg_emit("dprobs", FA[:, 4096:4224], [128, 128])
        self.dbg_emit("dcomb", FA[:, 4224:4352], [128, 128])
        def w1s(c, lo):
            if not lo:
                return BFA[:, c * 1024:(c + 1) * 1024]
            return BFA[:, 8192 + c * 1024: 9216 + c * 1024] if c < 4 else BFB[:, (c - 4) * 1024:(c - 3) * 1024]
        for e in range(8):
            base_mm_e = mmS.n
            def _lw(sy, e=e, base_mm_e=base_mm_e):
                sy.wait_ge(mmS.h, base_mm_e)
                if e == 0:
                    sy.wait_ge(gd.h, gd.n)
                for c in range(DC):
                    ld.dma(sy.dma_start(out=w1s(c, False), in_=w1h.ap()[e * 1024 + c * 128: e * 1024 + (c + 1) * 128, :]))
                    ld.dma(sy.dma_start(out=w1s(c, True), in_=w1l.ap()[e * 1024 + c * 128: e * 1024 + (c + 1) * 128, :]))
            self.blk.sync(_lw)
            base_ld_w = ld.n
            prev_mm_c = None
            prev_sc = None
            prev_gd = None
            for g2 in range(8):
                t0 = g2 * 256
                base_ld_g = ld.n
                def _lx(sy, t0=t0, prev=prev_mm_c):
                    for c in range(DC):
                        if prev is not None:
                            sy.wait_ge(mmS.h, prev[c])
                        ld.dma(sy.dma_start(out=BFB[:, 4096 + c * 256: 4352 + c * 256],
                                            in_=self.xh4_hi_sp.ap()[:, c * 2048 + t0: c * 2048 + t0 + 256]))
                        ld.dma(sy.dma_start(out=BFB[:, 6144 + c * 256: 6400 + c * 256],
                                            in_=self.xh4_lo_sp.ap()[:, c * 2048 + t0: c * 2048 + t0 + 256]))
                self.blk.sync(_lx)
                mm_c = []
                def _t(te, g2=g2, base_ld_g=base_ld_g, base_ld_w=base_ld_w, psc=prev_sc):
                    if g2 == 0:
                        te.wait_ge(ld.h, base_ld_w)
                        te.wait_ge(vv.h, vv.n)
                    if psc is not None:
                        te.wait_ge(sc.h, psc)
                    for c in range(DC):
                        te.wait_ge(ld.h, base_ld_g + 32 * (c + 1))
                        xhm = BFB[:, 4096 + c * 256: 4352 + c * 256]
                        xlm = BFB[:, 6144 + c * 256: 6400 + c * 256]
                        for fb in range(8):
                            psb = self.PS[fb][:, 0:256]
                            wh_s = w1s(c, False)[:, fb * 128:(fb + 1) * 128]
                            wl_s = w1s(c, True)[:, fb * 128:(fb + 1) * 128]
                            te.matmul(psb, wh_s, xhm, start=(c == 0), stop=False)
                            te.matmul(psb, wl_s, xhm, start=False, stop=False)
                            bi = te.matmul(psb, wh_s, xlm, start=False, stop=(c == DC - 1))
                        self.mmS.inc(bi)
                        mm_c.append(mmS.n)
                self.blk.tensor(_t)
                prev_mm_c = mm_c
                def _s(se, g2=g2, mm_after=mmS.n, pgd=prev_gd):
                    se.wait_ge(mmS.h, mm_after)
                    if pgd is not None:
                        se.wait_ge(gd.h, pgd)
                    for fb in range(8):
                        se.activation(FA[:, fb * 256:(fb + 1) * 256], self.PS[fb][:, 0:256], ACT.Silu)
                    self.sc.inc(se.drain())
                self.blk.scalar(_s)
                prev_sc = sc.n
                def _g(g, e=e, t0=t0, sc_after=sc.n):
                    g.wait_ge(sc.h, sc_after)
                    for fb in range(8):
                        gd.dma(g.dma_start(out=self.h_sp.ap()[:, e * 16384 + fb * 2048 + t0:
                                                              e * 16384 + fb * 2048 + t0 + 256],
                                           in_=FA[:, fb * 256:(fb + 1) * 256]))
                self.blk.gpsimd(_g)
                prev_gd = gd.n
        self.dbg_emit("dh0", self.h_sp.ap()[:, 0:16384], [128, 16384])
        for e in range(8):
            base_mm_e = mmS.n
            def _lw(sy, e=e, base_mm_e=base_mm_e):
                sy.wait_ge(mmS.h, base_mm_e)
                if e == 0:
                    sy.wait_ge(gd.h, gd.n)
                for c in range(DC):
                    ld.dma(sy.dma_start(out=w1s(c, False), in_=w2h.ap()[e * 1024 + c * 128: e * 1024 + (c + 1) * 128, :]))
                    ld.dma(sy.dma_start(out=w1s(c, True), in_=w2l.ap()[e * 1024 + c * 128: e * 1024 + (c + 1) * 128, :]))
            self.blk.sync(_lw)
            base_ld_w = ld.n
            prev_mm = None
            prev_vsp = None
            for g1 in range(16):
                t0 = g1 * 128
                base_ld_h = ld.n
                def _lh(sy, e=e, t0=t0, pv=prev_vsp):
                    if pv is not None:
                        sy.wait_ge(vv.h, pv)
                    for fb in range(8):
                        ld.dma(sy.dma_start(out=FA[:, fb * 128:(fb + 1) * 128],
                                            in_=self.h_sp.ap()[:, e * 16384 + fb * 2048 + t0:
                                                               e * 16384 + fb * 2048 + t0 + 128]))
                self.blk.sync(_lh)
                def _vsp(v, base_ld_h=base_ld_h, pm=prev_mm):
                    v.wait_ge(ld.h, base_ld_h + 128)
                    if pm is not None:
                        v.wait_ge(mmS.h, pm)
                    v.tensor_copy(BFB[:, 4096:5120], FA[:, 0:1024])
                    v.tensor_sub(BFB[:, 5120:6144], FA[:, 0:1024], BFB[:, 4096:5120])
                    self.vv.inc(v.drain())
                self.blk.vector(_vsp)
                prev_vsp = vv.n
                def _t(te, g1=g1, base_ld_w=base_ld_w, vsp_after=vv.n):
                    te.wait_ge(vv.h, vsp_after)
                    if g1 == 0:
                        te.wait_ge(ld.h, base_ld_w)
                    for c2 in range(DC):
                        hh = BFB[:, 4096 + c2 * 128: 4096 + (c2 + 1) * 128]
                        hl = BFB[:, 5120 + c2 * 128: 5120 + (c2 + 1) * 128]
                        for dh in range(2):
                            psb = self.PS[(g1 % 2) * 2 + dh][:, :]
                            wmh = w1s(c2, False)[:, dh * 512:(dh + 1) * 512]
                            wml = w1s(c2, True)[:, dh * 512:(dh + 1) * 512]
                            te.matmul(psb, hh, wmh, start=(c2 == 0), stop=False)
                            te.matmul(psb, hl, wmh, start=False, stop=False)
                            bi = te.matmul(psb, hh, wml, start=False, stop=(c2 == DC - 1))
                        self.mmS.inc(bi)
                self.blk.tensor(_t)
                prev_mm = mmS.n
                def _ve(v, e=e, g1=g1, mm_after=mmS.n):
                    v.wait_ge(mmS.h, mm_after)
                    ce = FA[:, 4224 + g1 * 8 + e: 4224 + g1 * 8 + e + 1]
                    for dh in range(2):
                        psb = self.PS[(g1 % 2) * 2 + dh][:, :]
                        dst = T2[:, g1 * 1024 + dh * 512: g1 * 1024 + (dh + 1) * 512]
                        if e == 0:
                            v.tensor_scalar_mul(dst, psb, ce)
                        else:
                            v.tensor_scalar_mul(FA[:, 1024:1536], psb, ce)
                            v.tensor_add(dst, dst, FA[:, 1024:1536])
                    self.vv.inc(v.drain())
                self.blk.vector(_ve)
        self.dbg_emit("drout", T2[:, :], [128, DC * TOK])
        dbgS = bool({"dsg", "dsgb", "dhsh"} & self.dbg)
        if dbgS:
            self.dsg_sp = self.dscr("dsg_sp", [1, 2080])
            self.dsgb_sp = self.dscr("dsgb_sp", [128, 2048])
            self.dhsh_sp = self.dscr("dhsh_sp", [128, 16384])
        prev_down_mm = None
        for g3 in range(4):
            t0 = g3 * 512
            base_ld_c = ld.n
            def _lc(sy, t0=t0, g3=g3, pdm=prev_down_mm):
                if g3 == 0:
                    sy.wait_ge(mmS.h, mmS.n)
                    sy.wait_ge(vv.h, vv.n)
                elif pdm is not None:
                    sy.wait_ge(mmS.h, pdm)
                for c in range(DC):
                    ld.dma(sy.dma_start(out=BFB[:, c * 512:(c + 1) * 512],
                                        in_=self.xh4_hi_sp.ap()[:, c * 2048 + t0: c * 2048 + t0 + 512]))
                    ld.dma(sy.dma_start(out=BFB[:, 4096 + c * 512: 4096 + (c + 1) * 512],
                                        in_=self.xh4_lo_sp.ap()[:, c * 2048 + t0: c * 2048 + t0 + 512]))
            self.blk.sync(_lc)
            base_ld_cd = ld.n
            for half in range(2):
                base_ld_w = ld.n; base_mm_w = mmS.n
                def _lwg(sy, half=half, base_mm_w=base_mm_w):
                    for c in range(DC):
                        if c < 2:
                            sy.wait_ge(mmS.h, base_mm_w)
                        else:
                            sy.wait_ge(mmS.h, base_mm_w + (c - 1))
                        ld.dma(sy.dma_start(out=self.WSL[2 * (c % 2)][:, :],
                                            in_=sguh.ap()[c * 128:(c + 1) * 128, half * 1024:(half + 1) * 1024]))
                        ld.dma(sy.dma_start(out=self.WSL[2 * (c % 2) + 1][:, :],
                                            in_=sgul.ap()[c * 128:(c + 1) * 128, half * 1024:(half + 1) * 1024]))
                self.blk.sync(_lwg)
                def _tg(te, half=half, base_ld_w=base_ld_w, base_ld_cd=base_ld_cd,
                        vnow=vv.n, snow=sc.n):
                    te.wait_ge(vv.h, vnow)
                    te.wait_ge(sc.h, snow)
                    te.wait_ge(ld.h, base_ld_cd)
                    for c in range(DC):
                        te.wait_ge(ld.h, base_ld_w + 32 * (c + 1))
                        whs = self.WSL[2 * (c % 2)]; wls = self.WSL[2 * (c % 2) + 1]
                        xhm = BFB[:, c * 512:(c + 1) * 512]
                        xlm = BFB[:, 4096 + c * 512: 4096 + (c + 1) * 512]
                        for jb in range(8):
                            psb = self.PS[jb][:, :]
                            te.matmul(psb, whs[:, jb * 128:(jb + 1) * 128], xhm, start=(c == 0), stop=False)
                            te.matmul(psb, wls[:, jb * 128:(jb + 1) * 128], xhm, start=False, stop=False)
                            bi = te.matmul(psb, whs[:, jb * 128:(jb + 1) * 128], xlm, start=False,
                                           stop=(c == DC - 1))
                        self.mmS.inc(bi)
                self.blk.tensor(_tg)
                if half == 0:
                    def _sg(se, mm_after=mmS.n):
                        se.wait_ge(mmS.h, mm_after)
                        for jb in range(8):
                            se.activation(FA[:, jb * 512:(jb + 1) * 512], self.PS[jb][:, :], ACT.Silu)
                        self.sc.inc(se.drain())
                    self.blk.scalar(_sg)
                else:
                    def _vu(v, mm_after=mmS.n, sc_after=sc.n):
                        v.wait_ge(mmS.h, mm_after)
                        v.wait_ge(sc.h, sc_after)
                        for jb in range(8):
                            v.tensor_mul(FA[:, jb * 512:(jb + 1) * 512], self.PS[jb][:, :],
                                         FA[:, jb * 512:(jb + 1) * 512])
                        self.vv.inc(v.drain())
                    self.blk.vector(_vu)
            def _vsp(v):
                v.tensor_copy(BFA[:, 0:4096], FA[:, 0:4096])
                v.tensor_sub(BFA[:, 4096:8192], FA[:, 0:4096], BFA[:, 0:4096])
                self.vv.inc(v.drain())
            self.blk.vector(_vsp)
            gd_shs = None
            if dbgS:
                def _gsh(g, t0=t0):
                    vv.wait(g)
                    gd.dma(g.dma_start(out=self.dhsh_sp.ap()[:, t0 * 8: t0 * 8 + 4096],
                                       in_=FA[:, 0:4096]))
                self.blk.gpsimd(_gsh)
                gd_shs = gd.n
            base_vvS = vv.n; base_mmS2 = mmS.n
            def _vsg(v, base_mmS2=base_mmS2, gd_shs=gd_shs):
                if gd_shs is not None:
                    v.wait_ge(gd.h, gd_shs)
                for c in range(DC):
                    if c >= 2:
                        v.wait_ge(mmS.h, base_mmS2 + (c - 1))
                    slot = FA[:, (c % 2) * 512:(c % 2) * 512 + 512]
                    v.tensor_add(slot, BFB[:, c * 512:(c + 1) * 512],
                                 BFB[:, 4096 + c * 512: 4096 + (c + 1) * 512])
                    v.drain()
                    v.tensor_scalar_mul(slot, slot, self.SGWF[:, c:c + 1])
                    self.vv.inc(v.drain())
            self.blk.vector(_vsg)
            def _tsg(te, base_vvS=base_vvS):
                for c in range(DC):
                    te.wait_ge(vv.h, base_vvS + c + 1)
                    slot = FA[:, (c % 2) * 512:(c % 2) * 512 + 512]
                    bi = te.matmul(self.PS[0][0:1, :], self.ONESF[:, 0:1], slot,
                                   start=(c == 0), stop=(c == DC - 1))
                    self.mmS.inc(bi)
            self.blk.tensor(_tsg)
            def _ssg(se, mm_after=mmS.n):
                se.wait_ge(mmS.h, mm_after)
                se.activation(self.SROW[0:1, 0:512], self.PS[0][0:1, :], ACT.Sigmoid)
                self.sc.inc(se.drain())
            self.blk.scalar(_ssg)
            if dbgS:
                def _gsg(g, t0=t0, sc_after=sc.n):
                    g.wait_ge(sc.h, sc_after)
                    gd.dma(g.dma_start(out=self.dsg_sp.ap()[0:1, t0:t0 + 512],
                                       in_=self.SROW[0:1, 0:512]))
                self.blk.gpsimd(_gsg)
            def _bsg(sy, sc_after=sc.n):
                sy.wait_ge(sc.h, sc_after)
                if dbgS:
                    sy.wait_ge(gd.h, gd.n)
                self.bcast_row(sy, self.SROW[0:1, 0:512], FA[:, 4608 - 512:4608], 512, 7)
            self.blk.sync(_bsg)
            base_ld_sgb = ld.n
            if dbgS:
                def _gsgb(g, t0=t0, base_ld_sgb=base_ld_sgb):
                    g.wait_ge(ld.h, base_ld_sgb)
                    gd.dma(g.dma_start(out=self.dsgb_sp.ap()[:, t0:t0 + 512],
                                       in_=FA[:, 4608 - 512:4608]))
                self.blk.gpsimd(_gsgb)
                gd_sgb = gd.n
            base_ld_w = ld.n; base_mm_w = mmS.n
            def _lwd(sy, base_mm_w=base_mm_w):
                for c in range(DC):
                    if c < 2:
                        sy.wait_ge(mmS.h, base_mm_w)
                    else:
                        sy.wait_ge(mmS.h, base_mm_w + (c - 1))
                    ld.dma(sy.dma_start(out=self.WSL[2 * (c % 2)][:, :],
                                        in_=sdwh.ap()[c * 128:(c + 1) * 128, :]))
                    ld.dma(sy.dma_start(out=self.WSL[2 * (c % 2) + 1][:, :],
                                        in_=sdwl.ap()[c * 128:(c + 1) * 128, :]))
            self.blk.sync(_lwd)
            def _td(te, base_ld_w=base_ld_w, vnow=vv.n, snow=sc.n):
                te.wait_ge(vv.h, vnow)
                te.wait_ge(sc.h, snow)
                for c in range(DC):
                    te.wait_ge(ld.h, base_ld_w + 32 * (c + 1))
                    whs = self.WSL[2 * (c % 2)]; wls = self.WSL[2 * (c % 2) + 1]
                    hhm = BFA[:, c * 512:(c + 1) * 512]
                    hlm = BFA[:, 4096 + c * 512: 4096 + (c + 1) * 512]
                    for db in range(8):
                        psb = self.PS[db][:, :]
                        te.matmul(psb, whs[:, db * 128:(db + 1) * 128], hhm, start=(c == 0), stop=False)
                        te.matmul(psb, wls[:, db * 128:(db + 1) * 128], hhm, start=False, stop=False)
                        bi = te.matmul(psb, whs[:, db * 128:(db + 1) * 128], hlm, start=False,
                                       stop=(c == DC - 1))
                    self.mmS.inc(bi)
            self.blk.tensor(_td)
            prev_down_mm = mmS.n
            def _vd(v, t0=t0, mm_after=mmS.n, base_ld_sgb=base_ld_sgb, gdn=gd.n):
                v.wait_ge(mmS.h, mm_after)
                v.wait_ge(ld.h, base_ld_sgb)
                if dbgS:
                    v.wait_ge(gd.h, gdn)
                for db in range(8):
                    v.tensor_mul(FA[:, 0:512], self.PS[db][:, :], FA[:, 4608 - 512:4608])
                    v.tensor_add(X[:, db * 2048 + t0: db * 2048 + t0 + 512],
                                 X[:, db * 2048 + t0: db * 2048 + t0 + 512], FA[:, 0:512])
                self.vv.inc(v.drain())
            self.blk.vector(_vd)
        if dbgS:
            self.dbg_emit("dsg", self.dsg_sp.ap(), [1, 2080])
            self.dbg_emit("dsgb", self.dsgb_sp.ap(), [128, 2048])
            self.dbg_emit("dhsh", self.dhsh_sp.ap(), [128, 16384])
        blocks = [(T2[:, g1 * 1024 + db * 128: g1 * 1024 + (db + 1) * 128],
                   X[:, db * 2048 + g1 * 128: db * 2048 + (g1 + 1) * 128])
                  for g1 in range(16) for db in range(DC)]
        self.emit_transposes(blocks, self.IDN, combine="add")

    def emit_out(self):
        gd = self.gd
        def _(g):
            for s_ in [self.vv, self.sc, self.mmS, self.gg, self.ld]:
                s_.wait(g)
            for c in range(DC):
                gd.dma(g.dma_start(out=self.y_out[c * 128:(c + 1) * 128, :],
                                   in_=self.X[:, c * TOK:(c + 1) * TOK]))
            gd.wait(g)
        self.blk.gpsimd(_)


def _vec_tile(v, chunks):
    return np.ascontiguousarray(np.asarray(v, np.float32).reshape(chunks, 128).T)


def make_shared_weights(I):
    f32 = lambda a: np.ascontiguousarray(np.asarray(a, np.float32))
    w = {}
    w["w_ln1_w"] = _vec_tile(I["L0_attn_ln1_w"], 8)
    w["w_ln1_b"] = _vec_tile(I["L0_attn_ln1_b"], 8)
    w["w_lns_w"] = _vec_tile(I["L0_attn_lns_w"], 8)
    w["w_lns_b"] = _vec_tile(I["L0_attn_lns_b"], 8)
    w["w_ln2_w"] = _vec_tile(I["L0_attn_ln2_w"], 8)
    w["w_ln2_b"] = _vec_tile(I["L0_attn_ln2_b"], 8)
    cw = np.asarray(I["L0_attn_conv_w"], np.float32).reshape(1024, 4)
    w["w_conv_w"] = np.ascontiguousarray(cw.reshape(8, 128, 4).transpose(1, 0, 2).reshape(128, 32))
    w["w_conv_b"] = _vec_tile(I["L0_attn_conv_b"], 8)
    w["w_in_w"] = f32(I["L0_attn_in_w"])
    w["w_in_b"] = _vec_tile(I["L0_attn_in_b"], 16)
    w["w_xp_w"] = f32(I["L0_attn_xp_w"])
    w["w_xp_b"] = f32(I["L0_attn_xp_b"]).reshape(16, 1)
    w["w_A_log"] = f32(I["L0_attn_A_log"]).reshape(16, 1)
    w["w_D_mat"] = f32(I["L0_attn_D"])
    w["w_out_w"] = f32(I["L0_attn_out_w"])
    w["w_out_b"] = _vec_tile(I["L0_attn_out_b"], 8)
    w["w_mlp_w1"] = f32(I["L0_attn_mlp_w1"])
    w["w_mlp_b1"] = _vec_tile(I["L0_attn_mlp_b1"], 16)
    w["w_mlp_w2"] = f32(I["L0_attn_mlp_w2"])
    w["w_mlp_b2"] = _vec_tile(I["L0_attn_mlp_b2"], 8)
    w["w_wg"] = f32(I["L0_ffn_w_gate"])
    w["w_wu"] = f32(I["L0_ffn_w_up"])
    w["w_wd"] = f32(I["L0_ffn_w_down"])
    w["w_WkT"] = np.ascontiguousarray(f32(I["L1_attn_W_k"]).T)
    w["w_WvT"] = np.ascontiguousarray(f32(I["L1_attn_W_v"]).T)
    w["w_WqT"] = np.ascontiguousarray(f32(I["L1_attn_W_q"]).T)
    w["w_router_w"] = f32(I["L1_ffn_router_w"])
    w["w_w1e"] = f32(I["L1_ffn_w1"]).reshape(8 * 1024, 1024)
    w["w_w2e"] = f32(I["L1_ffn_w2"]).reshape(8 * 1024, 1024)
    w["w_sgu"] = f32(I["L1_ffn_shared_gate_up"])
    w["w_sdw"] = f32(I["L1_ffn_shared_down"])
    w["w_sgw"] = f32(I["L1_ffn_shared_gate_w"]).reshape(1024, 1)
    w["c_idn"] = np.eye(128, dtype=np.float32)
    w["c_onesb"] = np.ones((128, 1), BF)
    w["c_onesf"] = np.ones((128, 1), np.float32)
    w["c_idnb"] = np.eye(128, dtype=BF)
    w["c_tpow"] = np.tile(np.arange(1, 513, dtype=np.float32)[None, :], (16, 1))
    w["c_eps"] = np.full((128, 1), LN_EPS, np.float32)
    w["c_xpw"] = np.ascontiguousarray(
        f32(I["L0_attn_xp_w"]).reshape(8, 128, 16).transpose(1, 0, 2).reshape(128, 128))
    w["c_rw"] = np.ascontiguousarray(
        f32(I["L1_ffn_router_w"]).reshape(8, 128, 8).transpose(1, 0, 2).reshape(128, 64))
    w["c_sgw"] = np.ascontiguousarray(f32(I["L1_ffn_shared_gate_w"]).reshape(8, 128).T)
    return w


def make_core_inputs(I, w, c):
    xf = np.asarray(I["L0_attn_x"], np.float32).reshape(16384, 1024)
    d = dict(w)
    xs = np.zeros((1024, 2 * SW), np.float32)
    for half in range(2):
        s0 = c * 1024
        lo = max(0, s0 - 3)
        npad = 3 - (s0 - lo)
        rows = np.arange(half * 8192 + lo, half * 8192 + s0 + 1024)
        xs[:, half * SW + npad:(half + 1) * SW] = xf[rows].T
    d["xs"] = xs
    xm = np.zeros((1024, 4 * MW), np.float32)
    for r, (b, s0) in enumerate([(0, c * 512), (1, c * 512), (0, 4096 + c * 512), (1, 4096 + c * 512)]):
        lo = max(0, s0 - 3)
        npad = 3 - (s0 - lo)
        rows = np.arange(b * 8192 + lo, b * 8192 + s0 + 512)
        xm[:, r * MW + npad:(r + 1) * MW] = xf[rows].T
    d["xm"] = xm
    pm = np.ones((128, 18), np.float32)
    if c == 0:
        pm[:, 0:3] = 0.0
        pm[:, 3:6] = 0.0
        pm[:, 6:9] = 0.0
        pm[:, 9:12] = 0.0
    d["c_padmask"] = pm
    rm = np.zeros((8, 1), np.float32); rm[c, 0] = 1.0
    d["c_rowmask"] = rm
    hs = np.zeros((16, 128), np.float32)
    for q in range(4):
        half, b = divmod(q, 2)
        jr = 8 * half + c
        if jr > 0:
            hs[:, q * 32 + b * 16 + (jr - 1)] = 1.0
    d["c_hsel"] = hs
    return d


def run_cores(I, dbg=(), cores=8):
    M = MK(dbg).build()
    w = make_shared_weights(I)
    in_maps = [make_core_inputs(I, w, c) for c in range(cores)]
    res = run_bass_kernel_spmd(M.nc, in_maps, core_ids=list(range(cores)))
    return M, res


def assemble(res, cores=8):
    full = np.zeros((16384, 1024), np.float32)
    for c in range(cores):
        y = res.results[c]["y_out"]
        r0 = np.arange(c * 1024, (c + 1) * 1024)
        rows = np.concatenate([r0, 8192 + r0])
        full[rows] = y.T
    return full.reshape(2, 8192, 1024)


def kernel(**inputs):
    M, res = run_cores(inputs)
    return assemble(res)
